# version 04
# baseline (speedup 1.0000x reference)
import contextlib
import os
import numpy as np
import concourse.bass as bass
import concourse.mybir as mybir
from concourse.bass_utils import run_bass_kernel_spmd

F32 = mybir.dt.float32
BF16 = mybir.dt.bfloat16
AF = mybir.ActivationFunctionType
ALU = mybir.AluOpType
AX = mybir.AxisListType

NT = 2048
D = 1024
NDK = 8
FFN = 4096
NE = 8
SG = 1024
NSG = NT // SG
NFC = 16
NP = 2048
LN_EPS = 1e-5
ATT_EPS = 1e-6

PHASES = int(os.environ.get("KERNEL_PHASES", "4"))


class Prog:
    ENGINES = ("tensor", "vector", "scalar", "sync", "gpsimd")

    def __init__(self, nc):
        self.nc = nc
        self.ops = {e: [] for e in self.ENGINES}
        self.cnt = {}
        self.sems = {}
        self.physcnt = {}
        self.released = set()
        self.virgin = set()
        self.dma_names = set()

    def sem(self, name):
        assert name not in self.released, f"sem {name} used after release"
        if name not in self.sems:
            h = self.nc.alloc_semaphore(name)
            base = self.physcnt.get(h.num, 0)
            self.sems[name] = (h, base)
            self.cnt[name] = 0
            self.virgin.add(name)
        return self.sems[name]

    def wait(self, eng, name, val):
        if val <= 0:
            return
        h, base = self.sem(name)
        self.ops[eng].append(lambda E, s=h, v=base + val: E.wait_ge(s, v))

    def wait_all(self, eng, name):
        self.wait(eng, name, self.cnt.get(name, 0))

    def inc_val(self, name, k=1):
        h, _ = self.sem(name)
        self.cnt[name] += k
        self.physcnt[h.num] = self.physcnt.get(h.num, 0) + k
        return self.cnt[name]

    def emit(self, eng, fn, inc=None, inck=1):
        if inc is not None:
            if inck == 16:
                self.dma_names.add(inc)
            h, base = self.sem(inc)
            if inc in self.virgin:
                self.virgin.discard(inc)
                if base > 0:
                    self.ops[eng].append(lambda E, s=h, v=base: E.wait_ge(s, v))
            self.inc_val(inc, inck)
            self.ops[eng].append(lambda E, fn=fn, s=h, k=inck: fn(E).then_inc(s, k))
        else:
            self.ops[eng].append(fn)

    def release(self, *prefixes):
        for name in list(self.sems):
            if any(name.startswith(pr) for pr in prefixes):
                h, _ = self.sems.pop(name)
                self.nc.release_semaphore(h)
                self.released.add(name)
                self.virgin.discard(name)

    nphase = 0

    def barrier_release(self, *prefixes):
        Prog.nphase += 1
        k = Prog.nphase
        names = [n for n in self.sems
                 if any(n.startswith(pr) for pr in prefixes)]
        for eng in self.ENGINES:
            for n in names:
                self.wait(eng, n, self.cnt.get(n, 0))
            if eng != "gpsimd":
                self.emit(eng, lambda E: E.nop(), inc=f"ack_{eng}")
        names = [n for n in names if n not in self.dma_names]
        for eng in self.ENGINES:
            if eng != "gpsimd":
                self.wait("gpsimd", f"ack_{eng}", k)
        handles = []
        for n in names:
            h, _ = self.sems.pop(n)
            handles.append(h)
            self.released.add(n)
            self.virgin.discard(n)
            self.physcnt[h.num] = 0
        for i, h in enumerate(handles):
            if i == len(handles) - 1:
                self.emit("gpsimd", lambda E, h=h: E.sem_clear(h), inc="phb")
            else:
                self.emit("gpsimd", lambda E, h=h: E.sem_clear(h))
        for h in handles:
            self.nc.release_semaphore(h)
        for eng in self.ENGINES:
            if eng != "gpsimd":
                self.wait(eng, "phb", k)

    def finalize(self):
        with self.nc.Block() as block:
            for eng in self.ENGINES:
                ops = self.ops[eng]
                if not ops:
                    continue
                getattr(block, eng)(lambda E, _ops=tuple(ops): [f(E) for f in _ops])


def build(core_id):
    nc = bass.Bass("TRN2", target_bir_lowering=False)
    p = Prog(nc)

    x_in = nc.dram_tensor("x", [NT, D], F32, kind="ExternalInput")
    rw_in = nc.dram_tensor("rw", [D, NE], F32, kind="ExternalInput")
    rb_in = nc.dram_tensor("rb", [1, NE], F32, kind="ExternalInput")
    w1_in = nc.dram_tensor("w1", [NE, D, 2 * FFN], F32, kind="ExternalInput")
    w2_in = nc.dram_tensor("w2", [NE, FFN, D], F32, kind="ExternalInput")
    wq_in = nc.dram_tensor("wq", [D, 3 * D], F32, kind="ExternalInput")
    phi_in = nc.dram_tensor("phi", [D, NP], F32, kind="ExternalInput")
    wf_in = nc.dram_tensor("wf", [64, D, D], F32, kind="ExternalInput")
    bf_in = nc.dram_tensor("bfb", [64, D], F32, kind="ExternalInput")
    y_out = nc.dram_tensor("y", [NT, D], F32, kind="ExternalOutput")

    kvt_send = nc.dram_tensor("kvt_send", [D, D + 8], F32)
    kvt_recv = nc.dram_tensor("kvt_recv", [D, D + 8], F32)
    e_send = nc.dram_tensor("e_send", [NT, NP], BF16)
    d_recv = nc.dram_tensor("d_recv", [NT, NP], BF16)
    s_send = nc.dram_tensor("s_send", [D, NP], BF16)
    s_recv = nc.dram_tensor("s_recv", [D, NP], BF16)
    wf_bf = nc.dram_tensor("wf_bf", [64, D, D], BF16)

    alloc = nc.alloc_sbuf_tensor
    x_fm = alloc("x_fm", [128, NDK * NT], F32)
    x_bf = alloc("x_bf", [128, NDK * NT], BF16)
    ident_f = alloc("ident_f", [128, 128], F32)
    ident_b = alloc("ident_b", [128, 128], BF16)
    onesf_r = alloc("onesf_r", [1, 128], F32)
    onesb_c = alloc("onesb_c", [128, 1], BF16)
    onesb_r = alloc("onesb_r", [1, 128], BF16)
    onesf_c = alloc("onesf_c", [128, 1], F32)
    cst = alloc("cst", [128, 4], F32)
    sel8 = alloc("sel8", [8, 8 * 128], BF16)
    pcol8 = alloc("pcol8", [8, 1], F32)
    sa_sb = [alloc(f"sa{i}", [128, 512], F32) for i in range(2)]
    tm_sb = [alloc(f"tmst{i}", [128, 512], F32) for i in range(2)]
    mbc_ = [alloc(f"mbc{i}", [128, 512], F32) for i in range(2)]
    att_kv = [alloc(f"akv{i}", [128, 512], BF16) for i in range(2)]
    kacc = alloc("kacc", [128, 1], F32)
    rw_bf = alloc("rw_bf", [128, NDK * NE], BF16)
    rb_row = alloc("rb_row", [1, NE], F32)
    rbb = alloc("rbb", [128, NE], F32)
    iota8 = alloc("iota8", [128, NE], F32)
    lg_tm = alloc("lg_tm", [128, 16 * NE], F32)
    mp_tm = alloc("mp_tm", [128, 16 * NE], F32)
    cols_a = alloc("cols_a", [128, 16], F32)
    cols_b = alloc("cols_b", [128, 8], F32)
    ks_col = alloc("ks_col", [128, 8], F32)
    ksb_col = alloc("ksb_col", [128, 8], BF16)
    bfl_bf = alloc("bfl_bf", [64, 1024], BF16)
    bo_bf = alloc("bo_bf", [128, 16 * 64], BF16)
    qmat = alloc("qmat", [128, 64], F32)
    boa = alloc("boa", [128, 64], F32)
    bob = alloc("bob", [128, 64], F32)
    iota64 = alloc("iota64", [128, 64], F32)
    cn_sb = alloc("cn_sb", [64, 512], BF16)

    ps_f = [nc.alloc_psum_tensor(f"ps{i}", [128, 512], F32) for i in range(6)]

    def TE(fn, inc=None):
        p.emit("tensor", fn, inc=inc)

    def VE(fn, inc=None):
        p.emit("vector", fn, inc=inc)

    def SC(fn, inc=None):
        p.emit("scalar", fn, inc=inc)

    def SY(fn, inc=None):
        p.emit("sync", fn, inc=inc, inck=16)

    def GP(fn, inc=None, inck=1):
        p.emit("gpsimd", fn, inc=inc, inck=inck)

    def GPD(fn, inc):
        p.emit("gpsimd", fn, inc=inc, inck=16)

    def mm_acc(bank_ap, ops, inc=None, tile_position=None):
        n = len(ops)
        for i, (l, r) in enumerate(ops):
            TE(lambda E, b=bank_ap, l=l, r=r, s=(i == 0), t=(i == n - 1),
               tp=tile_position: E.matmul(b, l, r, start=s, stop=t,
                                          tile_position=tp),
               inc=inc if i == n - 1 else None)

    def vsync(eng="vector"):
        p.emit(eng, lambda E: E.engine_nop(), inc=f"_vs_{eng}")
        p.wait(eng, f"_vs_{eng}", p.cnt[f"_vs_{eng}"])

    VE(lambda E: E.memset(ident_f[:, :], 0.0), inc="set0")
    VE(lambda E: E.memset(ident_b[:, :], 0.0), inc="set0")
    VE(lambda E: E.memset(onesf_r[:, :], 1.0), inc="set0")
    VE(lambda E: E.memset(onesb_c[:, :], 1.0), inc="set0")
    VE(lambda E: E.memset(onesf_c[:, :], 1.0), inc="set0")
    VE(lambda E: E.memset(onesb_r[:, :], 1.0), inc="set0")
    VE(lambda E: E.memset(cst[:, 0:1], LN_EPS), inc="set0")
    p.wait("gpsimd", "set0", 7)
    GP(lambda E: E.affine_select(
        out=ident_f[:, :], in_=ident_f[:, :], compare_op=ALU.not_equal,
        fill=1.0, base=0, pattern=[[-1, 128]], channel_multiplier=1), inc="setg")
    GP(lambda E: E.affine_select(
        out=ident_b[:, :], in_=ident_b[:, :], compare_op=ALU.not_equal,
        fill=1.0, base=0, pattern=[[-1, 128]], channel_multiplier=1), inc="setg")
    GP(lambda E: E.iota(iota8[:, :], [[1, NE]], channel_multiplier=0,
                        allow_small_or_imprecise_dtypes=True), inc="setg")
    GP(lambda E: E.iota(iota64[:, :], [[1, 64]], channel_multiplier=0,
                        allow_small_or_imprecise_dtypes=True), inc="setg")
    GP(lambda E: E.iota(pg32[:, 0:64] if False else qmat[:, :], [[-32, 64]],
                        channel_multiplier=1,
                        allow_small_or_imprecise_dtypes=True), inc="setg")
    GP(lambda E: E.iota(pcol8[:, :], [[0, 1]], channel_multiplier=1,
                        allow_small_or_imprecise_dtypes=True), inc="setg")
    selt_scope = nc.sbuf_tensor("selt", [8, 8 * 128], F32)
    selt = selt_scope.__enter__()
    GP(lambda E: E.iota(selt[:, :].rearrange("p (e x) -> p e x", e=8),
                        [[1, 8], [0, 128]], channel_multiplier=0,
                        allow_small_or_imprecise_dtypes=True), inc="setg")
    p.wait("vector", "setg", 7)
    VE(lambda E: E.tensor_scalar(sel8[:, :], selt[:, :], pcol8[:, 0:1],
                                 None, ALU.is_equal), inc="selg")
    selt_scope.__exit__(None, None, None)
    for ch in range(16):
        VE(lambda E, ch=ch: E.tensor_single_scalar(
            boa[:, :], qmat[:, :], float(-128 * ch), ALU.is_ge), inc="bo1")
        p.wait("vector", "bo1", p.cnt["bo1"])
        VE(lambda E, ch=ch: E.tensor_single_scalar(
            bob[:, :], qmat[:, :], float(32 - 128 * ch), ALU.is_lt), inc="bo1")
        p.wait("vector", "bo1", p.cnt["bo1"])
        VE(lambda E, ch=ch: E.tensor_mul(
            bo_bf[:, ch * 64:(ch + 1) * 64], boa[:, :], bob[:, :]), inc="bo2")

    GPD(lambda E: E.dma_start(
        out=rw_bf[:, :].rearrange("p (k e) -> p k e", k=NDK),
        in_=rw_in[:, :].rearrange("(k p) e -> p k e", p=128)), inc="rwl")
    GPD(lambda E: E.dma_start(out=rb_row[:, :], in_=rb_in[:, :]), inc="rbl")
    GPD(lambda E: E.dma_start(out=bfl_bf[:, :], in_=bf_in[:, :]), inc="bfll")

    ntr = 0
    with contextlib.ExitStack() as _st:
      stgi0 = _st.enter_context(nc.sbuf_tensor("stgi0", [128, 1024], F32))
      stgi1 = _st.enter_context(nc.sbuf_tensor("stgi1", [128, 1024], F32))
      stg = [stgi0, stgi1]
      for nm in ("selg", "setg", "set0", "pgs", "setg2"):
          p.wait("sync", nm, p.cnt.get(nm, 0))
      for mt in range(16):
        b = mt % 2
        if mt >= 2:
            p.wait("sync", "xevac", (mt - 1) * 8)
        SY(lambda E, b=b, mt=mt, stg=stg: E.dma_start(
            out=stg[b][:, :], in_=x_in[mt * 128:(mt + 1) * 128, :]), inc=f"xld{b}")
        p.wait("tensor", f"xld{b}", (mt // 2 + 1) * 16)
        if mt == 0:
            p.wait("tensor", "setg", 2)
        for dk in range(NDK):
            pb = ntr % 2
            p.wait("tensor", "xevac", ntr - 1)
            TE(lambda E, b=b, dk=dk, pb=pb, stg=stg: E.transpose(
                ps_f[pb][:, 0:128], stg[b][:, dk * 128:(dk + 1) * 128],
                ident_f[:, :]), inc="xtr")
            p.wait("vector", "xtr", ntr + 1)
            col = dk * NT + mt * 128
            VE(lambda E, pb=pb, col=col: E.tensor_copy(
                x_fm[:, col:col + 128], ps_f[pb][:, 0:128]))
            VE(lambda E, pb=pb, col=col: E.tensor_copy(
                x_bf[:, col:col + 128], ps_f[pb][:, 0:128]), inc="xevac")
            ntr += 1

    def layer_norm(tag):
        sq, st, ev, ro, rs, bc, ap, cz = (f"{tag}{s}" for s in
                                          ("sq", "st", "ev", "ro", "rs", "bc", "ap", "cz"))
        with contextlib.ExitStack() as _st:
            rows_a = _st.enter_context(nc.sbuf_tensor(f"{tag}rowa", [1, NT], F32))
            rows_b = _st.enter_context(nc.sbuf_tensor(f"{tag}rowb", [1, NT], F32))
            rows_c = _st.enter_context(nc.sbuf_tensor(f"{tag}rowc", [1, NT], F32))
            ln_sq = _st.enter_context(nc.sbuf_tensor(f"{tag}sqb", [128, NDK * 512], BF16))
            _layer_norm_body(tag, sq, st, ev, ro, rs, bc, ap, cz,
                             rows_a, rows_b, rows_c, ln_sq)

    def _layer_norm_body(tag, sq, st, ev, ro, rs, bc, ap, cz,
                         rows_a, rows_b, rows_c, ln_sq):
        for tg in range(4):
            for dj in range(NDK):
                col = dj * NT + tg * 512
                VE(lambda E, col=col: E.tensor_copy(
                    x_bf[:, col:col + 512], x_fm[:, col:col + 512]), inc=cz)
            p.wait("scalar", cz, p.cnt[cz])
            if tg >= 1:
                p.wait("scalar", st, 2 * tg)
            for dj in range(NDK):
                col = dj * NT + tg * 512
                SC(lambda E, dj=dj, col=col: E.activation(
                    ln_sq[:, dj * 512:(dj + 1) * 512], x_bf[:, col:col + 512],
                    AF.Square), inc=sq)
            p.wait("tensor", cz, p.cnt[cz])
            p.wait("tensor", sq, p.cnt[sq])
            if tg >= 1:
                p.wait("tensor", ev, 2 * tg)
            mm_acc(ps_f[4][0:1, :],
                   [(onesb_c[:, :], x_bf[:, dj * NT + tg * 512:dj * NT + tg * 512 + 512])
                    for dj in range(NDK)], inc=st)
            mm_acc(ps_f[5][0:1, :],
                   [(onesb_c[:, :], ln_sq[:, dj * 512:(dj + 1) * 512])
                    for dj in range(NDK)], inc=st)
            p.wait("vector", st, p.cnt[st])
            c0 = tg * 512
            VE(lambda E, c0=c0: E.tensor_scalar_mul(
                rows_a[:, c0:c0 + 512], ps_f[4][0:1, :], 1.0 / D), inc=ev)
            VE(lambda E, c0=c0: E.tensor_scalar_mul(
                rows_b[:, c0:c0 + 512], ps_f[5][0:1, :], 1.0 / D), inc=ev)
        p.wait("vector", ev, p.cnt[ev])
        VE(lambda E: E.tensor_mul(rows_c[:, :], rows_a[:, :], rows_a[:, :]), inc=ro)
        p.wait("vector", ro, p.cnt[ro])
        VE(lambda E: E.tensor_sub(rows_b[:, :], rows_b[:, :], rows_c[:, :]), inc=ro)
        p.wait("scalar", ro, p.cnt[ro])
        p.wait("scalar", "set0", 6)
        SC(lambda E: E.activation(rows_c[:, :], rows_b[:, :], AF.Ln,
                                  bias=cst[0:1, 0:1]), inc=rs)
        p.wait("scalar", rs, p.cnt[rs])
        SC(lambda E: E.activation(rows_b[:, :], rows_c[:, :], AF.Exp,
                                  scale=-0.5), inc=rs)
        for tg in range(4):
            bmean = ps_f[(tg % 2) * 2 + 0]
            brstd = ps_f[(tg % 2) * 2 + 1]
            p.wait("tensor", rs, p.cnt[rs])
            if tg >= 2:
                p.wait("tensor", ap, (tg - 1) * 16)
            c0 = tg * 512
            TE(lambda E, b=bmean, c0=c0: E.matmul(
                b[:, :], onesf_r[:, :], rows_a[0:1, c0:c0 + 512],
                start=True, stop=True), inc=bc)
            TE(lambda E, b=brstd, c0=c0: E.matmul(
                b[:, :], onesf_r[:, :], rows_b[0:1, c0:c0 + 512],
                start=True, stop=True), inc=bc)
            p.wait("vector", bc, p.cnt[bc])
            for dj in range(NDK):
                col = dj * NT + tg * 512
                sb = dj % 2
                VE(lambda E, col=col, b=bmean, sb=sb: E.tensor_sub(
                    tm_sb[sb][:, :], x_fm[:, col:col + 512], b[:, :]))
                vsync()
                VE(lambda E, col=col, b=brstd, sb=sb: E.tensor_mul(
                    x_fm[:, col:col + 512], tm_sb[sb][:, :], b[:, :]), inc=ap)
                vsync()
                VE(lambda E, col=col: E.tensor_copy(
                    x_bf[:, col:col + 512], x_fm[:, col:col + 512]), inc=ap)
        p.wait("tensor", ap, p.cnt[ap])


    def phase_barrier(tag, dma_sems=()):
        for eng in ("gpsimd", "sync", "scalar", "vector", "tensor"):
            p.wait(eng, f"{tag}ap", p.cnt[f"{tag}ap"])
            for ds in dma_sems:
                p.wait(eng, ds, p.cnt.get(ds, 0))

    if PHASES >= 1:
        with contextlib.ExitStack() as _st:
            pb_t0 = _st.enter_context(nc.psum_tensor("pb_t0", [128, 512], BF16))
            pb_t1 = _st.enter_context(nc.psum_tensor("pb_t1", [128, 512], BF16))
            xev = _st.enter_context(nc.sbuf_tensor("xev", [128, 16 * 512], BF16))
            ao_fm = _st.enter_context(nc.sbuf_tensor("ao_fm", [128, 16 * 512], F32))
            att_sb = _st.enter_context(nc.sbuf_tensor("att_sb", [128, 512], BF16))
            attT0 = _st.enter_context(nc.sbuf_tensor("attT0", [128, 512], BF16))
            attT1 = _st.enter_context(nc.sbuf_tensor("attT1", [128, 512], BF16))
            xs_tm = _st.enter_context(nc.sbuf_tensor("xs_tm", [128, 256], BF16))
            arow = _st.enter_context(nc.sbuf_tensor("arow", [1, 512], F32))
            nmax = _st.enter_context(nc.sbuf_tensor("nmax", [128, 2], F32))
            rsum = _st.enter_context(nc.sbuf_tensor("rsum", [128, 2], F32))
            pb_t = [pb_t0, pb_t1]
            p.wait("vector", "xevac", 128)
            for dj in range(NDK):
                for si in range(2):
                    src0 = dj * NT + si * 1024
                    dst0 = (dj * 2 + si) * 512
                    VE(lambda E, src0=src0, dst0=dst0: E.tensor_copy(
                        xev[:, dst0:dst0 + 512], x_fm[:, src0:src0 + 1024:2]),
                       inc="p1ev")
            nsh = 0
            for si in range(2):
                for hh in range(16):
                    dj = hh // 2
                    hp = (hh % 2) * 64
                    hcol = (dj * 2 + si) * 512
                    tp_out = (0, hp) if hp else None
                    if nsh == 0:
                        p.wait("tensor", "p1ev", 16)
                        p.wait("tensor", "xevac", 128)
                    p.wait("tensor", "p1xs", nsh)
                    for j in range(4):
                        TE(lambda E, hp=hp, hcol=hcol, j=j: E.transpose(
                            pb_t0[:, j * 64:(j + 1) * 64],
                            xev[hp:hp + 64, hcol + j * 128:hcol + (j + 1) * 128],
                            ident_b[hp:hp + 64, hp:hp + 64]),
                           inc="p1tx" if j == 3 else None)
                    p.wait("vector", "p1tx", p.cnt["p1tx"])
                    VE(lambda E: E.tensor_copy(xs_tm[:, 0:256], pb_t0[:, 0:256]),
                       inc="p1xs")
                    for qs in range(4):
                        k = nsh * 4 + qs
                        sbank = ps_f[k % 2]
                        c = k % 2
                        p.wait("tensor", "p1at", k - 1)
                        TE(lambda E, hp=hp, hcol=hcol, qs=qs, b=sbank: E.matmul(
                            b[:, :],
                            xev[hp:hp + 64, hcol + qs * 128:hcol + qs * 128 + 128],
                            xev[hp:hp + 64, hcol:hcol + 512],
                            start=True, stop=True), inc="p1sc")
                        p.wait("vector", "p1sc", k + 1)
                        VE(lambda E, b=sbank, c=c: E.reduce_max(
                            nmax[:, c:c + 1], b[:, :], axis=AX.X, negate=True),
                           inc="p1mx")
                        p.wait("vector", "p1mx", p.cnt["p1mx"])
                        VE(lambda E, c=c: E.tensor_scalar_mul(
                            nmax[:, c:c + 1], nmax[:, c:c + 1], 0.125), inc="p1m2")
                        p.wait("scalar", "p1m2", p.cnt["p1m2"])
                        p.wait("scalar", "p1at", k - 1)
                        SC(lambda E, b=sbank, c=c: E.activation(
                            sa_sb[c][:, :], b[:, :], AF.Exp, scale=0.125,
                            bias=nmax[:, c:c + 1], accum_out=rsum[:, c:c + 1]),
                           inc="p1ex")
                        p.wait("vector", "p1ex", p.cnt["p1ex"])
                        VE(lambda E, c=c: E.reciprocal(
                            rsum[:, c:c + 1], rsum[:, c:c + 1]), inc="p1rc")
                        p.wait("vector", "p1rc", p.cnt["p1rc"])
                        p.wait("vector", "p1tt", p.cnt.get("p1tt", 0))
                        VE(lambda E, c=c: E.tensor_scalar_mul(
                            att_sb[:, :], sa_sb[c][:, :], rsum[:, c:c + 1]),
                           inc="p1at")
                        p.wait("tensor", "p1at", p.cnt["p1at"])
                        p.wait("tensor", "p1te", k - 1)
                        for j in range(4):
                            TE(lambda E, j=j: E.transpose(
                                pb_t1[:, j * 128:(j + 1) * 128],
                                att_sb[:, j * 128:(j + 1) * 128], ident_b[:, :]),
                               inc="p1tt" if j == 3 else None)
                        attT = [attT0, attT1][k % 2]
                        p.wait("vector", "p1tt", p.cnt["p1tt"])
                        p.wait("vector", "p1ou", k - 1)
                        VE(lambda E, attT=attT: E.tensor_copy(
                            attT[:, :], pb_t1[:, :]), inc="p1te")
                        obank = ps_f[2 + (nsh % 2)]
                        p.wait("tensor", "p1te", k + 1)
                        if qs == 0:
                            p.wait("tensor", "p1oe", nsh - 1)
                        for j in range(4):
                            TE(lambda E, j=j, o=obank, hp=hp, qs=qs, tp=tp_out,
                               attT=attT:
                               E.matmul(o[hp:hp + 64, qs * 128:qs * 128 + 128],
                                        xs_tm[:, j * 64:(j + 1) * 64],
                                        attT[:, j * 128:(j + 1) * 128],
                                        start=(j == 0), stop=(j == 3),
                                        tile_position=tp), inc="p1ou" if j == 3 else None)
                    p.wait("vector", "p1ou", p.cnt["p1ou"])
                    VE(lambda E, hp=hp, o=ps_f[2 + (nsh % 2)], dj=dj, si=si:
                       E.tensor_copy(
                           ao_fm[hp:hp + 64, (dj * 2 + si) * 512:(dj * 2 + si) * 512 + 512],
                           o[hp:hp + 64, :]), inc="p1oe")
                    nsh += 1
            for si in range(2):
                p.wait("tensor", "p1oe", 32)
                p.wait("tensor", "p1rr", p.cnt.get("p1rr", 0))
                mm_acc(ps_f[4][0:1, :],
                       [(onesf_c[:, :], ao_fm[:, (dj * 2 + si) * 512:(dj * 2 + si) * 512 + 512])
                        for dj in range(NDK)], inc="p1dn")
                p.wait("vector", "p1dn", p.cnt["p1dn"])
                VE(lambda E: E.tensor_copy(arow[:, :], ps_f[4][0:1, :]), inc="p1al")
                vsync()
                VE(lambda E: E.reciprocal(arow[:, :], arow[:, :]), inc="p1al")
                vsync()
                VE(lambda E: E.tensor_mul(arow[:, :], ps_f[4][0:1, :], arow[:, :]),
                   inc="p1al")
                p.wait("tensor", "p1al", p.cnt["p1al"])
                TE(lambda E: E.matmul(ps_f[5][:, :], onesf_r[:, :], arow[0:1, :],
                                      start=True, stop=True), inc="p1ab")
                p.wait("vector", "p1ab", p.cnt["p1ab"])
                for dj in range(NDK):
                    sb = dj % 2
                    acol = (dj * 2 + si) * 512
                    xcol = dj * NT + si * 1024
                    VE(lambda E, acol=acol, sb=sb: E.tensor_mul(
                        tm_sb[sb][:, :], ao_fm[:, acol:acol + 512], ps_f[5][:, :]))
                    vsync()
                    VE(lambda E, xcol=xcol, sb=sb: E.tensor_add(
                        x_fm[:, xcol:xcol + 1024:2], x_fm[:, xcol:xcol + 1024:2],
                        tm_sb[sb][:, :]), inc="p1rr")
                p.wait("tensor", "p1rr", p.cnt["p1rr"])
        layer_norm("n1")
        p.barrier_release("p1", "n1", "xld", "xtr", "xevac")

    if PHASES >= 2:
        with contextlib.ExitStack() as _st:
            psy0 = _st.enter_context(nc.psum_tensor("psy0", [128, 512], F32))
            psy1 = _st.enter_context(nc.psum_tensor("psy1", [128, 512], F32))
            u_buf = _st.enter_context(nc.sbuf_tensor("u_buf", [128, 32 * SG], BF16))
            w1a0 = _st.enter_context(nc.sbuf_tensor("w1a0", [128, 2048], BF16))
            w1a1 = _st.enter_context(nc.sbuf_tensor("w1a1", [128, 2048], BF16))
            w1g0 = _st.enter_context(nc.sbuf_tensor("w1g0", [128, 2048], BF16))
            w1g1 = _st.enter_context(nc.sbuf_tensor("w1g1", [128, 2048], BF16))
            w2b0 = _st.enter_context(nc.sbuf_tensor("w2b0", [128, 512], BF16))
            w2b1 = _st.enter_context(nc.sbuf_tensor("w2b1", [128, 512], BF16))
            m_fm = _st.enter_context(nc.sbuf_tensor("m_fm", [8, NT], BF16))
            et8 = _st.enter_context(nc.sbuf_tensor("et8", [128, 8], F32))
            z8 = _st.enter_context(nc.sbuf_tensor("z8", [128, 8], F32))
            m8 = _st.enter_context(nc.sbuf_tensor("m8", [128, 8], F32))
            idx8 = _st.enter_context(nc.sbuf_tensor("idx8", [128, 8], mybir.dt.uint32))
            mcol = _st.enter_context(nc.sbuf_tensor("mcol", [128, 4], F32))
            w1a = [w1a0, w1a1]
            w1g = [w1g0, w1g1]
            w2b = [w2b0, w2b1]
            ps8 = ps_f + [psy0, psy1]
            p.wait("tensor", "rwl", 16)
            for mt in range(16):
                bank = ps_f[4] if mt % 2 == 0 else ps_f[5]
                if mt >= 2:
                    p.wait("tensor", "p2lg", mt - 1)
                mm_acc(bank[:, 0:8],
                       [(x_bf[:, dk * NT + mt * 128:dk * NT + mt * 128 + 128],
                         rw_bf[:, dk * 8:(dk + 1) * 8]) for dk in range(NDK)],
                       inc="p2lgm")
                p.wait("vector", "p2lgm", p.cnt["p2lgm"])
                VE(lambda E, mt=mt, b=bank: E.tensor_copy(
                    lg_tm[:, mt * 8:(mt + 1) * 8], b[:, 0:8]), inc="p2lg")
            p.wait("tensor", "rbl", 16)
            p.wait("tensor", "p2lg", 16)
            TE(lambda E: E.matmul(ps_f[4][:, 0:8], onesf_r[:, :], rb_row[:, :],
                                  start=True, stop=True), inc="p2rb")
            p.wait("vector", "p2rb", 1)
            VE(lambda E: E.tensor_copy(rbb[:, :], ps_f[4][:, 0:8]), inc="p2rbe")
            vsync()
            for mt in range(16):
                VE(lambda E, mt=mt: E.tensor_add(
                    lg_tm[:, mt * 8:(mt + 1) * 8], lg_tm[:, mt * 8:(mt + 1) * 8],
                    rbb[:, :]), inc="p2lb")
            VE(lambda E: E.memset(z8[:, :], 0.0), inc="p2z8")
            p.wait("vector", "p2z8", 1)
            for mt in range(16):
                p.wait("vector", "p2lb", 16)
                VE(lambda E, mt=mt: E.reduce_max(
                    mcol[:, 0:1], lg_tm[:, mt * 8:(mt + 1) * 8], axis=AX.X),
                   inc="p2w")
                vsync()
                VE(lambda E: E.tensor_scalar_mul(mcol[:, 1:2], mcol[:, 0:1], -1.0),
                   inc="p2w")
                p.wait("scalar", "p2w", p.cnt["p2w"])
                SC(lambda E, mt=mt: E.activation(
                    et8[:, 0:8], lg_tm[:, mt * 8:(mt + 1) * 8], AF.Exp,
                    bias=mcol[:, 1:2], accum_out=mcol[:, 2:3]), inc="p2e")
                p.wait("vector", "p2e", p.cnt["p2e"])
                VE(lambda E: E.reciprocal(mcol[:, 2:3], mcol[:, 2:3]), inc="p2w")
                vsync()
                VE(lambda E: E.tensor_scalar_add(m8[:, :], z8[:, :],
                                                 mcol[:, 0:1]), inc="p2w")
                vsync()
                VE(lambda E, mt=mt: E.max_index(
                    idx8[:, :], m8[:, :], lg_tm[:, mt * 8:(mt + 1) * 8]),
                   inc="p2w")
                vsync()
                VE(lambda E: E.tensor_copy(mcol[:, 3:4], idx8[:, 0:1]), inc="p2w")
                vsync()
                VE(lambda E, mt=mt: E.tensor_scalar(
                    mp_tm[:, mt * 8:(mt + 1) * 8], iota8[:, :], mcol[:, 3:4],
                    None, ALU.is_equal), inc="p2w")
                vsync()
                VE(lambda E, mt=mt: E.tensor_scalar(
                    mp_tm[:, mt * 8:(mt + 1) * 8], mp_tm[:, mt * 8:(mt + 1) * 8],
                    mcol[:, 2:3], None, ALU.mult), inc="p2mp")
            for mt in range(16):
                p.wait("tensor", "p2mp", 16)
                p.wait("tensor", "p2mf", mt)
                TE(lambda E, mt=mt: E.transpose(
                    ps_f[4][0:8, 0:128], mp_tm[:, mt * 8:(mt + 1) * 8],
                    ident_f[:, :]), inc="p2mt")
                p.wait("vector", "p2mt", p.cnt["p2mt"])
                VE(lambda E, mt=mt: E.tensor_copy(
                    m_fm[0:8, mt * 128:(mt + 1) * 128], ps_f[4][0:8, 0:128]),
                   inc="p2mf")

            ia = 0
            nw1 = 0
            nw2 = 0
            mb = 0
            npass = 0
            nsg_done = 0
            nwf = 0
            w2_marks = []

            def w1_load(e, fc):
                nonlocal nw1, nwf
                par = nw1 % 2
                if nw1 >= 2:
                    kprev = nw1 - 2
                    p.wait("gpsimd", "agrp", (kprev + 1) * 4)
                    p.wait("gpsimd", "ggrp", (kprev + 1) * 4)
                for half, buf in ((0, w1a), (1, w1g)):
                    base = half * FFN + fc * 256
                    src = w1_in[e, :, base:base + 256].rearrange("(k p) f -> p k f", p=128)
                    dst = buf[par][:, :].rearrange("p (k f) -> p k f", k=NDK)
                    GPD(lambda E, src=src, dst=dst: E.dma_start(out=dst, in_=src),
                        inc=f"w1l{par}")
                nw1 += 1
                if PHASES >= 4 and nw1 % 4 == 0 and nwf < 64:
                    n = nwf
                    GPD(lambda E, n=n: E.dma_start(
                        out=wf_bf[n, :, :], in_=wf_in[n, :, :]), inc="wfc")
                    nwf += 1

            def w2_load(e, fi, rnd):
                nonlocal nw2
                par = nw2 % 2
                if nw2 >= 2:
                    sname, sval = w2_marks[nw2 - 2]
                    p.wait("gpsimd", sname, sval)
                src = w2_in[e, fi * 128:(fi + 1) * 128, rnd * 512:(rnd + 1) * 512]
                GPD(lambda E, src=src, par=par: E.dma_start(
                    out=w2b[par][:, 0:512], in_=src), inc=f"w2l{par}")
                nw2 += 1

            blocks = [(e, sg, fc) for e in range(NE) for sg in range(NSG)
                      for fc in range(NFC)]
            w1_load(0, 0)
            w1_load(0, 1)

            for e in range(NE):
                for sg in range(NSG):
                    gsg = e * NSG + sg
                    for fc in range(NFC):
                        bi = gsg * NFC + fc
                        if bi + 2 < len(blocks):
                            w1_load(blocks[bi + 2][0], blocks[bi + 2][2])
                        for sub in range(2):
                            for th in range(2):
                                abk = ia % 2
                                p.wait("tensor", f"w1l{bi % 2}", (bi // 2 + 1) * 32)
                                p.wait("tensor", "p2sil", ia - 1)
                                p.wait("tensor", "p2um", ia - 1)
                                if fc == 0 and sub == 0 and th == 0:
                                    p.wait("tensor", "p2mbk", mb)
                                    p.wait("tensor", "p2ym", npass * 8)
                                for half in range(2):
                                    tgt = ps_f[abk] if half == 0 else ps_f[2 + abk]
                                    wb = w1a if half == 0 else w1g
                                    grp = "agrp" if half == 0 else "ggrp"
                                    for dk in range(NDK):
                                        lcol = dk * 256 + sub * 128
                                        rcol = dk * NT + sg * SG + th * 512
                                        TE(lambda E, tgt=tgt, wb=wb, lcol=lcol,
                                           rcol=rcol, par=bi % 2, s=(dk == 0),
                                           t=(dk == 7): E.matmul(
                                            tgt[:, :], wb[par][:, lcol:lcol + 128],
                                            x_bf[:, rcol:rcol + 512],
                                            start=s, stop=t),
                                           inc=grp if dk == 7 else None)
                                p.wait("scalar", "agrp", ia + 1)
                                p.wait("scalar", "p2um", ia - 1)
                                SC(lambda E, abk=abk: E.activation(
                                    sa_sb[abk][:, :], ps_f[abk][:, :], AF.Silu),
                                   inc="p2sil")
                                fi = fc * 2 + sub
                                ucol = fi * SG + th * 512
                                p.wait("vector", "p2sil", ia + 1)
                                p.wait("vector", "ggrp", ia + 1)
                                if ia % 64 == 0 and gsg > 0:
                                    p.wait("vector", "p2yb", nsg_done * 2)
                                VE(lambda E, abk=abk, ucol=ucol: E.tensor_mul(
                                    u_buf[:, ucol:ucol + 512], sa_sb[abk][:, :],
                                    ps_f[2 + abk][:, :]), inc="p2um")
                                ia += 1
                    p.wait("tensor", "p2um", ia)
                    p.wait("tensor", "p2sil", ia)
                    for th in range(2):
                        c0 = sg * SG + th * 512
                        p.wait("tensor", "p2mbk", mb)
                        if e == 0 and sg == 0 and th == 0:
                            p.wait("tensor", "selg", 1)
                        TE(lambda E, c0=c0, e=e: E.matmul(
                            ps_f[0][:, :], sel8[:, e * 128:(e + 1) * 128],
                            m_fm[0:8, c0:c0 + 512],
                            start=True, stop=True), inc="p2mbc")
                        p.wait("vector", "p2mbc", mb + 1)
                        VE(lambda E, b=mb % 2: E.tensor_copy(
                            mbc_[b][:, :], ps_f[0][:, :]), inc="p2mbk")
                        mb += 1
                    mbth = {0: (mb - 2) % 2, 1: (mb - 1) % 2}
                    p.wait("tensor", "p2mbk", mb)
                    for rnd in range(2):
                        for fi in range(32):
                            w2_load(e, fi, rnd)
                            p.wait("tensor", f"w2l{(nw2 - 1) % 2}",
                                   ((nw2 - 1) // 2 + 1) * 16)
                            for dcq in range(4):
                                for th in range(2):
                                    bank = dcq * 2 + th
                                    if fi == 0:
                                        p.wait("tensor", "p2ym",
                                               (npass - 1) * 8 + bank + 1)
                                    ucol = fi * SG + th * 512
                                    wcol = dcq * 128
                                    last = (fi == 31 and bank == 7)
                                    inc = ("p2yb" if last else
                                           ("p2wu" if bank == 7 else None))
                                    TE(lambda E, bank=bank, wcol=wcol, ucol=ucol,
                                       par=(nw2 - 1) % 2, s=(fi == 0),
                                       t=(fi == 31): E.matmul(
                                        ps8[bank][:, :], w2b[par][:, wcol:wcol + 128],
                                        u_buf[:, ucol:ucol + 512], start=s, stop=t),
                                       inc=inc)
                                    if bank == 7:
                                        w2_marks.append(
                                            ("p2yb", p.cnt["p2yb"]) if last
                                            else ("p2wu", p.cnt["p2wu"]))
                        p.wait("vector", "p2yb", npass + 1)
                        for dcq in range(4):
                            for th in range(2):
                                bank = dcq * 2 + th
                                dc = rnd * 4 + dcq
                                xcol = dc * NT + sg * SG + th * 512
                                sb = bank % 2
                                VE(lambda E, bank=bank, sb=sb, mbb=mbth[th]:
                                   E.tensor_mul(tm_sb[sb][:, :], ps8[bank][:, :],
                                                mbc_[mbb][:, :]), inc="p2ym")
                                vsync()
                                VE(lambda E, sb=sb, xcol=xcol: E.tensor_add(
                                    x_fm[:, xcol:xcol + 512],
                                    x_fm[:, xcol:xcol + 512], tm_sb[sb][:, :]),
                                   inc="p2ya")
                        npass += 1
                    nsg_done += 1
            p.wait("tensor", "p2ya", p.cnt["p2ya"])
        layer_norm("n2")
        p.barrier_release("p2", "n2", "w1l", "w2l", "agrp", "ggrp", "rwl", "rbl")

    if PHASES >= 3:
        with contextlib.ExitStack() as _st:
            pb_t0 = _st.enter_context(nc.psum_tensor("pb_t0a", [128, 512], BF16))
            pb_t1 = _st.enter_context(nc.psum_tensor("pb_t1a", [128, 512], BF16))
            k_tm = _st.enter_context(nc.sbuf_tensor("k_tm", [128, 16 * 1024], BF16))
            v_tm = _st.enter_context(nc.sbuf_tensor("v_tm", [128, 16 * 1024], BF16))
            wqb0 = _st.enter_context(nc.sbuf_tensor("wqb0", [128, 4096], BF16))
            wqb1 = _st.enter_context(nc.sbuf_tensor("wqb1", [128, 4096], BF16))
            wsb = [wqb0, wqb1]
            pb_t = [pb_t0, pb_t1]
            VE(lambda E: E.memset(ks_col[:, :], 0.0), inc="p3ks")
            nblk = 0
            nkv = 0
            for part, colbase in (("k", D), ("v", 2 * D)):
                for cg in range(2):
                    par = nblk % 2
                    if nblk >= 2:
                        p.wait("gpsimd", "p3kv", (nblk - 1) * 16)
                    src = wq_in[:, colbase + cg * 512:colbase + (cg + 1) * 512]
                    GPD(lambda E, src=src, buf=wsb[par]: E.dma_start(
                        out=buf[:, :].rearrange("p (k f) -> p k f", k=NDK),
                        in_=src.rearrange("(k p) f -> p k f", p=128)),
                        inc=f"p3w{par}")
                    nblk += 1
                    for cs in range(4):
                        for tg in range(4):
                            bank = ps_f[nkv % 2]
                            sb = nkv % 2
                            p.wait("tensor", f"p3w{par}",
                                   ((nblk - 1) // 2 + 1) * 16)
                            p.wait("tensor", "p3ke", nkv - 1)
                            mm_acc(bank[:, :],
                                   [(wsb[par][:, dk * 512 + cs * 128:dk * 512 + cs * 128 + 128],
                                     x_bf[:, dk * NT + tg * 512:dk * NT + tg * 512 + 512])
                                    for dk in range(NDK)], inc="p3kv")
                            kc = cg * 4 + cs
                            p.wait("vector", "p3kv", p.cnt["p3kv"])
                            p.wait("vector", "p3tt", nkv - 1)
                            if part == "k":
                                VE(lambda E, b=bank, sb=sb: E.tensor_scalar_min(
                                    sa_sb[sb][:, :], b[:, :], 0.0), inc="p3mn")
                                p.wait("scalar", "p3mn", p.cnt["p3mn"])
                                p.wait("scalar", "p3ke", nkv - 1)
                                SC(lambda E, sb=sb: E.activation(
                                    tm_sb[sb][:, :], sa_sb[sb][:, :], AF.Exp),
                                   inc="p3ex")
                                p.wait("vector", "p3ex", p.cnt["p3ex"])
                                VE(lambda E, b=bank, sb=sb: E.tensor_scalar_max(
                                    sa_sb[sb][:, :], b[:, :], 0.0), inc="p3mx")
                                vsync()
                                VE(lambda E, sb=sb: E.scalar_tensor_tensor(
                                    att_kv[sb][:, :], tm_sb[sb][:, :], 0.0,
                                    sa_sb[sb][:, :], ALU.add, ALU.add,
                                    accum_out=kacc[:, 0:1]), inc="p3ke")
                                vsync()
                                VE(lambda E, kc=kc: E.tensor_add(
                                    ks_col[:, kc:kc + 1], ks_col[:, kc:kc + 1],
                                    kacc[:, 0:1]), inc="p3ks")
                            else:
                                VE(lambda E, b=bank, sb=sb: E.tensor_copy(
                                    att_kv[sb][:, :], b[:, :]), inc="p3ke")
                            dst = k_tm if part == "k" else v_tm
                            p.wait("tensor", "p3ke", p.cnt["p3ke"])
                            p.wait("tensor", "p3te", (nkv - 1) * 4)
                            for j in range(4):
                                TE(lambda E, j=j, b=pb_t[sb], sb=sb:
                                   E.transpose(b[:, j * 128:(j + 1) * 128],
                                               att_kv[sb][:, j * 128:(j + 1) * 128],
                                               ident_b[:, :]),
                                   inc="p3tt" if j == 3 else None)
                            p.wait("vector", "p3tt", p.cnt["p3tt"])
                            for j in range(4):
                                tcol = (tg * 4 + j) * 1024 + kc * 128
                                VE(lambda E, j=j, b=pb_t[sb], tcol=tcol, dst=dst:
                                   E.tensor_copy(dst[:, tcol:tcol + 128],
                                                 b[:, j * 128:(j + 1) * 128]),
                                   inc="p3te")
                            nkv += 1
            nkt = 0
            p.wait("tensor", "p3te", p.cnt["p3te"])
            for dc in range(8):
                for hg in range(2):
                    bank = ps_f[2 + (nkt % 2)]
                    sb = nkt % 2
                    p.wait("tensor", "p3kte", nkt - 1)
                    mm_acc(bank[:, :],
                           [(k_tm[:, lc * 1024 + dc * 128:lc * 1024 + dc * 128 + 128],
                             v_tm[:, lc * 1024 + hg * 512:lc * 1024 + hg * 512 + 512])
                            for lc in range(16)], inc="p3kt")
                    p.wait("vector", "p3kt", p.cnt["p3kt"])
                    if nkt >= 2:
                        p.wait("vector", f"p3sd{sb}", ((nkt - 2) // 2 + 1) * 16)
                    VE(lambda E, b=bank, sb=sb: E.tensor_copy(
                        tm_sb[sb][:, :], b[:, :]), inc="p3kte")
                    p.wait("sync", "p3kte", p.cnt["p3kte"])
                    SY(lambda E, dc=dc, hg=hg, sb=sb: E.dma_start(
                        out=kvt_send[dc * 128:(dc + 1) * 128, hg * 512:(hg + 1) * 512],
                        in_=tm_sb[sb][:, :]), inc=f"p3sd{sb}")
                    nkt += 1
            p.wait("sync", "p3ks", p.cnt["p3ks"])
            for cc in range(8):
                SY(lambda E, cc=cc: E.dma_start(
                    out=kvt_send[:, D + cc:D + cc + 1].rearrange(
                        "(k p) c -> p k c", p=128),
                    in_=ks_col[:, :].rearrange("p (k c) -> p k c", c=1),
                    allow_slow_non_contiguous=True),
                   inc="p3sd0")
            p.wait("gpsimd", "p3sd0", p.cnt["p3sd0"])
            p.wait("gpsimd", "p3sd1", p.cnt["p3sd1"])
            GP(lambda E: E.collective_compute(
                "AllReduce", ALU.add,
                replica_groups=[[0, 1], [2, 3], [4, 5], [6, 7]],
                ins=[kvt_send[:, :].opt()], outs=[kvt_recv[:, :].opt()]),
               inc="p3cc")

        with contextlib.ExitStack() as _st:
            pb_t0b = _st.enter_context(nc.psum_tensor("pb_t0b", [128, 512], BF16))
            pb_t1b = _st.enter_context(nc.psum_tensor("pb_t1b", [128, 512], BF16))
            q_bf = _st.enter_context(nc.sbuf_tensor("q_bf", [128, NDK * NT], BF16))
            kvt_bf = _st.enter_context(nc.sbuf_tensor("kvt_bf", [128, 8 * 1024], BF16))
            wqc0 = _st.enter_context(nc.sbuf_tensor("wqc0", [128, 4096], BF16))
            wqc1 = _st.enter_context(nc.sbuf_tensor("wqc1", [128, 4096], BF16))
            zrow = _st.enter_context(nc.sbuf_tensor("zrow", [1, NT], F32))
            wsb = [wqc0, wqc1]
            nblk = 0
            nq = 0
            p.wait("gpsimd", "p3kv", 64)
            for cg in range(2):
                par = nblk % 2
                if nblk >= 2:
                    p.wait("gpsimd", "p3q", (nblk - 1) * 16)
                src = wq_in[:, cg * 512:(cg + 1) * 512]
                GPD(lambda E, src=src, buf=wsb[par]: E.dma_start(
                    out=buf[:, :].rearrange("p (k f) -> p k f", k=NDK),
                    in_=src.rearrange("(k p) f -> p k f", p=128)), inc=f"p3qw{par}")
                nblk += 1
                for cs in range(4):
                    for tg in range(4):
                        bank = ps_f[nq % 2]
                        p.wait("tensor", f"p3qw{par}", ((nblk - 1) // 2 + 1) * 16)
                        p.wait("tensor", "p3qe", nq - 1)
                        mm_acc(bank[:, :],
                               [(wsb[par][:, dk * 512 + cs * 128:dk * 512 + cs * 128 + 128],
                                 x_bf[:, dk * NT + tg * 512:dk * NT + tg * 512 + 512])
                                for dk in range(NDK)], inc="p3q")
                        qc = cg * 4 + cs
                        sb = nq % 2
                        p.wait("vector", "p3q", p.cnt["p3q"])
                        p.wait("vector", "p3qx", p.cnt.get("p3qx", 0))
                        VE(lambda E, b=bank, sb=sb: E.tensor_scalar_min(
                            sa_sb[sb][:, :], b[:, :], 0.0), inc="p3qmn")
                        p.wait("scalar", "p3qmn", p.cnt["p3qmn"])
                        SC(lambda E, sb=sb: E.activation(
                            tm_sb[sb][:, :], sa_sb[sb][:, :], AF.Exp), inc="p3qx")
                        p.wait("vector", "p3qx", p.cnt["p3qx"])
                        VE(lambda E, b=bank, sb=sb: E.tensor_scalar_max(
                            sa_sb[sb][:, :], b[:, :], 0.0), inc="p3qmx")
                        vsync()
                        qcol = qc * NT + tg * 512
                        VE(lambda E, sb=sb, qcol=qcol: E.scalar_tensor_tensor(
                            q_bf[:, qcol:qcol + 512], tm_sb[sb][:, :], 0.0,
                            sa_sb[sb][:, :], ALU.add, ALU.add), inc="p3qe")
                        nq += 1
            p.wait("gpsimd", "p3cc", 1)
            for dc in range(8):
                GPD(lambda E, dc=dc: E.dma_start(
                    out=kvt_bf[:, dc * 1024:(dc + 1) * 1024],
                    in_=kvt_recv[dc * 128:(dc + 1) * 128, 0:D]), inc="p3rb")
            GPD(lambda E: E.dma_start(
                out=ksb_col[:, :].rearrange("p (k c) -> p k c", c=1),
                in_=kvt_recv[:, D:D + 1].rearrange("(k p) c -> p k c", p=128),
                allow_slow_non_contiguous=True),
                inc="p3rb")
            for tg in range(4):
                bank = ps_f[4]
                p.wait("tensor", "p3rb", 9 * 16)
                p.wait("tensor", "p3qe", p.cnt["p3qe"])
                p.wait("tensor", "p3ze", p.cnt.get("p3ze", 0))
                mm_acc(bank[0:1, :],
                       [(ksb_col[:, dc:dc + 1],
                         q_bf[:, dc * NT + tg * 512:dc * NT + tg * 512 + 512])
                        for dc in range(8)], inc="p3z")
                p.wait("vector", "p3z", p.cnt["p3z"])
                c0 = tg * 512
                VE(lambda E, c0=c0: E.tensor_scalar_add(
                    zrow[:, c0:c0 + 512], ps_f[4][0:1, :], ATT_EPS), inc="p3ze")
            vsync()
            VE(lambda E: E.reciprocal(zrow[:, :], zrow[:, :]), inc="p3zr")
            nv = 0
            for tg in range(4):
                p.wait("tensor", "p3zr", 1)
                p.wait("tensor", "p3zc", tg)
                TE(lambda E, tg=tg: E.matmul(
                    ps_f[5][:, :], onesf_r[:, :], zrow[0:1, tg * 512:tg * 512 + 512],
                    start=True, stop=True), inc="p3zb")
                p.wait("vector", "p3zb", tg + 1)
                p.wait("vector", "p3ve", (tg - 1) * 8)
                VE(lambda E, zc=mbc_[tg % 2]: E.tensor_copy(
                    zc[:, :], ps_f[5][:, :]), inc="p3zc")
                for hc in range(8):
                    bank = ps_f[2 + (nv % 2)]
                    p.wait("tensor", "p3ve", nv - 1)
                    mm_acc(bank[:, :],
                           [(kvt_bf[:, dc * 1024 + hc * 128:dc * 1024 + hc * 128 + 128],
                             q_bf[:, dc * NT + tg * 512:dc * NT + tg * 512 + 512])
                            for dc in range(8)], inc="p3v")
                    sb = nv % 2
                    p.wait("vector", "p3v", p.cnt["p3v"])
                    p.wait("vector", "p3zc", tg + 1)
                    VE(lambda E, b=bank, sb=sb, zc=mbc_[tg % 2]: E.tensor_mul(
                        tm_sb[sb][:, :], b[:, :], zc[:, :]), inc="p3ve")
                    vsync()
                    xcol = hc * NT + tg * 512
                    VE(lambda E, sb=sb, xcol=xcol: E.tensor_add(
                        x_fm[:, xcol:xcol + 512], x_fm[:, xcol:xcol + 512],
                        tm_sb[sb][:, :]), inc="p3vr")
                    nv += 1
            p.wait("tensor", "p3vr", p.cnt["p3vr"])
        layer_norm("n3")
        p.barrier_release("p3", "n3")

    if PHASES >= 4:
        with nc.sbuf_tensor("phi_bf", [128, NDK * NP], BF16) as phi_bf:
            for dk in range(NDK):
                GPD(lambda E, dk=dk: E.dma_start(
                    out=phi_bf[:, dk * NP:(dk + 1) * NP],
                    in_=phi_in[dk * 128:(dk + 1) * 128, :]), inc="p4pw")
            for mt in range(16):
                for npg in range(4):
                    k = mt * 4 + npg
                    bank = ps_f[k % 2]
                    sb = k % 2
                    p.wait("tensor", "p4pw", 8 * 16)
                    p.wait("tensor", "p4ee", k - 1)
                    mm_acc(bank[:, :],
                           [(x_bf[:, dk * NT + mt * 128:dk * NT + mt * 128 + 128],
                             phi_bf[:, dk * NP + npg * 512:dk * NP + npg * 512 + 512])
                            for dk in range(NDK)], inc="p4lg")
                    p.wait("scalar", "p4lg", k + 1)
                    p.wait("scalar", "p4ca", k - 1)
                    if k >= 2:
                        p.wait("scalar", f"p4es{sb}", (k // 2) * 16)
                    SC(lambda E, b=bank, sb=sb: E.activation(
                        att_kv[sb][:, :], b[:, :], AF.Exp,
                        accum_out=kacc[:, 0:1]), inc="p4ee")
                    p.wait("vector", "p4ee", k + 1)
                    if npg == 0:
                        VE(lambda E, mt=mt: E.tensor_copy(
                            cols_a[:, mt:mt + 1], kacc[:, 0:1]), inc="p4ca")
                    else:
                        VE(lambda E, mt=mt: E.tensor_add(
                            cols_a[:, mt:mt + 1], cols_a[:, mt:mt + 1],
                            kacc[:, 0:1]), inc="p4ca")
                    p.wait("sync", "p4ee", k + 1)
                    SY(lambda E, mt=mt, npg=npg, sb=sb: E.dma_start(
                        out=e_send[mt * 128:(mt + 1) * 128, npg * 512:(npg + 1) * 512],
                        in_=att_kv[sb][:, :]), inc=f"p4es{sb}")
            p.wait("vector", "p4ca", 64)
            VE(lambda E: E.reciprocal(cols_a[:, :], cols_a[:, :]), inc="p4cr")
            p.wait("gpsimd", "p4es0", p.cnt["p4es0"])
            p.wait("gpsimd", "p4es1", p.cnt["p4es1"])
            GP(lambda E: E.collective_compute(
                "AllReduce", ALU.add,
                replica_groups=[[0, 2, 4, 6], [1, 3, 5, 7]],
                ins=[e_send[:, :].opt()], outs=[d_recv[:, :].opt()]),
               inc="p4cc1")

        with contextlib.ExitStack() as _st:
            pb_t0c = _st.enter_context(nc.psum_tensor("pb_t0c", [128, 512], BF16))
            pb_t1c = _st.enter_context(nc.psum_tensor("pb_t1c", [128, 512], BF16))
            x_tm = _st.enter_context(nc.sbuf_tensor("x_tm", [128, 16 * 1024], BF16))
            est_e = _st.enter_context(nc.sbuf_tensor("est_e", [128, 16 * 512], BF16))
            est_d = _st.enter_context(nc.sbuf_tensor("est_d", [128, 16 * 512], BF16))
            disp = _st.enter_context(nc.sbuf_tensor("disp", [128, 16 * 512], BF16))
            pb_tc = [pb_t0c, pb_t1c]
            nxt = 0
            for mt in range(16):
                for dc in range(NDK):
                    sb = nxt % 2
                    p.wait("tensor", "p4xe", nxt - 1)
                    TE(lambda E, mt=mt, dc=dc, b=pb_tc[sb]: E.transpose(
                        b[:, 0:128], x_bf[:, dc * NT + mt * 128:dc * NT + mt * 128 + 128],
                        ident_b[:, :]), inc="p4xt")
                    p.wait("vector", "p4xt", nxt + 1)
                    VE(lambda E, mt=mt, dc=dc, b=pb_tc[sb]: E.tensor_copy(
                        x_tm[:, mt * 1024 + dc * 128:mt * 1024 + dc * 128 + 128],
                        b[:, 0:128]), inc="p4xe")
                    nxt += 1
            for npg in range(4):
                p.wait("gpsimd", "p4cc1", 1)
                if npg:
                    p.wait("gpsimd", "p4dp", npg * 16)
                GPD(lambda E, npg=npg: E.dma_start(
                    out=est_d[:, :].rearrange("p (k c) -> p k c", k=16),
                    in_=d_recv[:, npg * 512:(npg + 1) * 512].rearrange(
                        "(k p) c -> p k c", p=128)), inc="p4dr")
                GPD(lambda E, npg=npg: E.dma_start(
                    out=est_e[:, :].rearrange("p (k c) -> p k c", k=16),
                    in_=e_send[:, npg * 512:(npg + 1) * 512].rearrange(
                        "(k p) c -> p k c", p=128)), inc="p4er")
                for lm in range(16):
                    sb = lm % 2
                    if lm == 0 and npg > 0:
                        p.wait("vector", "p4sl", npg * 8)
                    p.wait("scalar", "p4dr", (npg + 1) * 16)
                    p.wait("scalar", "p4dp", npg * 16 + lm - 1)
                    SC(lambda E, lm=lm, sb=sb: E.activation(
                        tm_sb[sb][:, :], est_d[:, lm * 512:(lm + 1) * 512],
                        AF.Ln), inc="p4ln")
                    p.wait("scalar", "p4ln", p.cnt["p4ln"])
                    SC(lambda E, sb=sb: E.activation(
                        sa_sb[sb][:, :], tm_sb[sb][:, :], AF.Exp, scale=-1.0),
                       inc="p4ex")
                    p.wait("vector", "p4ex", p.cnt["p4ex"])
                    p.wait("vector", "p4er", (npg + 1) * 16)
                    VE(lambda E, lm=lm, sb=sb: E.tensor_mul(
                        disp[:, lm * 512:(lm + 1) * 512],
                        est_e[:, lm * 512:(lm + 1) * 512], sa_sb[sb][:, :]),
                       inc="p4dp")
                for dc in range(NDK):
                    k = npg * 8 + dc
                    bank = ps_f[2 + (k % 2)]
                    sb = k % 2
                    p.wait("tensor", "p4dp", (npg + 1) * 16)
                    p.wait("tensor", "p4xe", 128)
                    p.wait("tensor", "p4se", k - 1)
                    mm_acc(bank[:, :],
                           [(x_tm[:, mt * 1024 + dc * 128:mt * 1024 + dc * 128 + 128],
                             disp[:, mt * 512:(mt + 1) * 512]) for mt in range(16)],
                           inc="p4sl")
                    p.wait("vector", "p4sl", k + 1)
                    if k >= 2:
                        p.wait("vector", f"p4ss{sb}", (k // 2) * 16)
                    VE(lambda E, b=bank, sb=sb: E.tensor_copy(
                        att_kv[sb][:, :], b[:, :]), inc="p4se")
                    p.wait("sync", "p4se", k + 1)
                    SY(lambda E, dc=dc, npg=npg, sb=sb: E.dma_start(
                        out=s_send[dc * 128:(dc + 1) * 128, npg * 512:(npg + 1) * 512],
                        in_=att_kv[sb][:, :]), inc=f"p4ss{sb}")
            p.wait("gpsimd", "p4ss0", p.cnt["p4ss0"])
            p.wait("gpsimd", "p4ss1", p.cnt["p4ss1"])
            GP(lambda E: E.collective_compute(
                "AllReduce", ALU.add,
                replica_groups=[[0, 1], [2, 3], [4, 5], [6, 7]],
                ins=[s_send[:, :].opt()], outs=[s_recv[:, :].opt()]),
               inc="p4cc2")

        slt_bf = x_bf
        with contextlib.ExitStack() as _sty:
          y_bf = _sty.enter_context(nc.sbuf_tensor("y_bf", [128, 16 * 1024], BF16))
          with contextlib.ExitStack() as _st:
            wfb0 = _st.enter_context(nc.sbuf_tensor("wfb0", [128, 8192], BF16))
            wfb1 = _st.enter_context(nc.sbuf_tensor("wfb1", [128, 8192], BF16))
            wsb = [wfb0, wfb1]
            p.wait("gpsimd", "p4cc2", 1)
            for dc in range(NDK):
                GPD(lambda E, dc=dc: E.dma_start(
                    out=slt_bf[:, dc * NP:(dc + 1) * NP],
                    in_=s_recv[dc * 128:(dc + 1) * 128, :]), inc="p4sr")
            for n in range(64):
                par = n % 2
                if n >= 2:
                    p.wait("gpsimd", "p4yg", (n - 1) * 2)
                p.wait("gpsimd", "wfc", 64 * 16)
                GPD(lambda E, n=n, buf=wsb[par]: E.dma_start(
                    out=buf[:, :].rearrange("p (k e) -> p k e", k=NDK),
                    in_=wf_bf[n, :, :].rearrange("(k p) e -> p k e", p=128)),
                    inc=f"p4wf{par}")
            for n in range(64):
                par = n % 2
                j = n % 4
                q = n // 4
                for eg in range(2):
                    bank = ps_f[2 + 2 * (q % 2) + eg]
                    if n == 0 and eg == 0:
                        p.wait("tensor", "p4sr", 8 * 16)
                    p.wait("tensor", f"p4wf{par}", (n // 2 + 1) * 16)
                    if q >= 2 and eg == 0 and j == 0:
                        p.wait("tensor", "p4ye", (q - 1) * 8)
                    mm_acc(bank[32 * j:32 * (j + 1), :],
                           [(slt_bf[:, dk * NP + n * 32:dk * NP + n * 32 + 32],
                             wsb[par][:, dk * 1024 + eg * 512:dk * 1024 + eg * 512 + 512])
                            for dk in range(NDK)], inc="p4yg",
                           tile_position=(0, 32 * j))
                if j == 3:
                    p.wait("vector", "p4yg", (n + 1) * 2)
                    for jj in range(4):
                        for eg in range(2):
                            bank = ps_f[2 + 2 * (q % 2) + eg]
                            nn = q * 4 + jj
                            VE(lambda E, b=bank, jj=jj, q=q, eg=eg: E.tensor_copy(
                                y_bf[32 * jj:32 * (jj + 1),
                                     q * 1024 + eg * 512:q * 1024 + eg * 512 + 512],
                                b[32 * jj:32 * (jj + 1), :]), inc="p4ye")

          with contextlib.ExitStack() as _st:
                pb_t0d = _st.enter_context(nc.psum_tensor("pb_t0d", [128, 512], BF16))
                pb_t1d = _st.enter_context(nc.psum_tensor("pb_t1d", [128, 512], BF16))
                est2 = _st.enter_context(nc.sbuf_tensor("est2", [128, 4 * NP], BF16))
                comb_t = _st.enter_context(nc.sbuf_tensor("comb_t", [128, 16 * 512], BF16))
                pb_td = [pb_t0d, pb_t1d]
                for eng in ("gpsimd", "vector", "tensor", "scalar", "sync"):
                    for nm in ("p4wf0", "p4wf1", "p4yg", "p4ye", "p4sr", "p4xe"):
                        p.wait(eng, nm, p.cnt.get(nm, 0))
                for mg in range(4):
                    if mg:
                        p.wait("gpsimd", "p4ob", mg * 8)
                    GPD(lambda E, mg=mg: E.dma_start(
                        out=est2[:, :].rearrange("p (k c) -> p k c", k=4),
                        in_=e_send[mg * 512:(mg + 1) * 512, :].rearrange(
                            "(k p) c -> p k c", p=128)), inc="p4e2")
                    nct = 0
                    for lm in range(4):
                        mt = mg * 4 + lm
                        for npg in range(4):
                            sb = nct % 2
                            p.wait("vector", "p4e2", (mg + 1) * 16)
                            p.wait("vector", "p4cr", 1)
                            p.wait("vector", "p4tt", p.cnt.get("p4tt", 0))
                            VE(lambda E, lm=lm, npg=npg, sb=sb, mt=mt:
                               E.tensor_scalar(
                                att_kv[sb][:, :],
                                est2[:, lm * NP + npg * 512:lm * NP + npg * 512 + 512],
                                cols_a[:, mt:mt + 1], None, ALU.mult), inc="p4cm")
                            p.wait("tensor", "p4cm", p.cnt["p4cm"])
                            p.wait("tensor", "p4ct", p.cnt.get("p4ct", 0) - 4)
                            for jj in range(4):
                                TE(lambda E, jj=jj, sb=sb, b=pb_td[sb]: E.transpose(
                                    b[:, jj * 128:(jj + 1) * 128],
                                    att_kv[sb][:, jj * 128:(jj + 1) * 128],
                                    ident_b[:, :]), inc="p4tt" if jj == 3 else None)
                            p.wait("vector", "p4tt", p.cnt["p4tt"])
                            for jj in range(4):
                                ccol = (npg * 4 + jj) * 512 + lm * 128
                                VE(lambda E, jj=jj, sb=sb, b=pb_td[sb], ccol=ccol:
                                   E.tensor_copy(comb_t[:, ccol:ccol + 128],
                                                 b[:, jj * 128:(jj + 1) * 128]),
                                   inc="p4ct")
                            nct += 1
                    p.wait("tensor", "p4ct", p.cnt["p4ct"])
                    p.wait("tensor", "bo2", 16)
                    p.wait("tensor", "p4ye", 128)
                    p.wait("tensor", "p4cne", mg)
                    mm_acc(ps_f[4][0:64, :],
                           [(bo_bf[:, npc * 64:(npc + 1) * 64],
                             comb_t[:, npc * 512:(npc + 1) * 512])
                            for npc in range(16)], inc="p4cn")
                    p.wait("vector", "p4cn", p.cnt["p4cn"])
                    VE(lambda E: E.tensor_copy(cn_sb[:, :], ps_f[4][0:64, :]),
                       inc="p4cne")
                    for dc in range(NDK):
                        k = mg * 8 + dc
                        bank = ps_f[k % 2]
                        p.wait("tensor", "p4cne", mg + 1)
                        p.wait("tensor", "bfll", 16)
                        p.wait("tensor", "p4ob", k - 1)
                        ops = [(bfl_bf[:, dc * 128:(dc + 1) * 128], cn_sb[:, :])]
                        ops += [(y_bf[:, npc * 1024 + dc * 128:npc * 1024 + dc * 128 + 128],
                                 comb_t[:, npc * 512:(npc + 1) * 512])
                                for npc in range(16)]
                        mm_acc(bank[:, :], ops, inc="p4ou")
                        p.wait("vector", "p4ou", k + 1)
                        xcol = dc * NT + mg * 512
                        VE(lambda E, b=bank, xcol=xcol: E.tensor_add(
                            x_fm[:, xcol:xcol + 512], x_fm[:, xcol:xcol + 512],
                            b[:, :]), inc="p4ob")
                p.wait("tensor", "p4ob", p.cnt["p4ob"])
        layer_norm("n4")
        p.barrier_release("p4", "n4")

    not_ = 0
    with contextlib.ExitStack() as _st:
      stgo = [_st.enter_context(nc.sbuf_tensor(f"stgo{i}", [128, 1024], F32))
              for i in range(2)]
      for mt in range(16):
        b = mt % 2
        for dk in range(NDK):
            pb = not_ % 2
            p.wait("tensor", "oevac", not_ - 1)
            col = dk * NT + mt * 128
            TE(lambda E, pb=pb, col=col: E.transpose(
                ps_f[pb][:, 0:128], x_fm[:, col:col + 128], ident_f[:, :]),
               inc="otr")
            p.wait("vector", "otr", not_ + 1)
            if mt >= 2 and dk == 0:
                p.wait("vector", f"ost{b}", (mt // 2) * 16)
            VE(lambda E, pb=pb, b=b, dk=dk, t=stgo[b]: E.tensor_copy(
                t[:, dk * 128:(dk + 1) * 128], ps_f[pb][:, 0:128]),
               inc="oevac")
            not_ += 1
        p.wait("sync", "oevac", not_)
        SY(lambda E, b=b, mt=mt, t=stgo[b]: E.dma_start(
            out=y_out[mt * 128:(mt + 1) * 128, :], in_=t[:, :]),
           inc=f"ost{b}")
      p.wait("sync", "ost0", 8 * 16)
      p.wait("sync", "ost1", 8 * 16)

    p.finalize()
    return nc


def emit_phase4(nc, p, L):
    raise NotImplementedError


_NC_CACHE = {}


def kernel(**inputs):
    x = np.ascontiguousarray(np.asarray(inputs["L0_attn_x"], dtype=np.float32))
    rw = np.ascontiguousarray(np.asarray(inputs["L0_ffn_router_w"], dtype=np.float32))
    rb = np.ascontiguousarray(np.asarray(inputs["L0_ffn_router_b"], dtype=np.float32)).reshape(1, NE)
    w1 = np.ascontiguousarray(np.asarray(inputs["L0_ffn_w1"], dtype=np.float32))
    w2 = np.ascontiguousarray(np.asarray(inputs["L0_ffn_w2"], dtype=np.float32))
    wq = np.ascontiguousarray(np.asarray(inputs["L1_attn_W"], dtype=np.float32))
    phi = np.ascontiguousarray(np.asarray(inputs["L1_ffn_phi"], dtype=np.float32)).reshape(D, NP)
    wf = np.ascontiguousarray(np.asarray(inputs["L1_ffn_W"], dtype=np.float32))
    bfb = np.ascontiguousarray(np.asarray(inputs["L1_ffn_b"], dtype=np.float32))

    if "nc" not in _NC_CACHE:
        _NC_CACHE["nc"] = build(0)
    ncp = _NC_CACHE["nc"]

    in_maps = []
    for c in range(8):
        b, h = c // 2, c % 2
        in_maps.append({
            "x": np.ascontiguousarray(x[b, h * NT:(h + 1) * NT, :]),
            "rw": rw, "rb": rb, "w1": w1, "w2": w2,
            "wq": wq, "phi": phi, "wf": wf, "bfb": bfb,
        })
    res = run_bass_kernel_spmd(ncp, in_maps, core_ids=list(range(8)))
    out = np.empty((4, 4096, D), dtype=np.float32)
    for c in range(8):
        b, h = c // 2, c % 2
        out[b, h * NT:(h + 1) * NT, :] = res.results[c]["y"]
    return out


# version 05 (final)
# speedup vs baseline: 1.0078x; 1.0078x over previous
import contextlib
import os
import numpy as np
import concourse.bass as bass
import concourse.bass_utils as _bass_utils
import concourse.mybir as mybir
from concourse.bass_utils import run_bass_kernel_spmd

if not getattr(_bass_utils, "_ldw_opt_patched", False):
    _orig_run_command = _bass_utils.run_command

    def _run_command_ldw(cmd, **kw):
        cmd = ["--enable-ldw-opt=true" if c == "--enable-ldw-opt=false" else c
               for c in cmd]
        return _orig_run_command(cmd, **kw)

    _bass_utils.run_command = _run_command_ldw
    _bass_utils._ldw_opt_patched = True

F32 = mybir.dt.float32
BF16 = mybir.dt.bfloat16
AF = mybir.ActivationFunctionType
ALU = mybir.AluOpType
AX = mybir.AxisListType

NT = 2048
D = 1024
NDK = 8
FFN = 4096
NE = 8
SG = 1024
NSG = NT // SG
NFC = 16
NP = 2048
LN_EPS = 1e-5
ATT_EPS = 1e-6

PHASES = int(os.environ.get("KERNEL_PHASES", "4"))


class Prog:
    ENGINES = ("tensor", "vector", "scalar", "sync", "gpsimd")

    def __init__(self, nc):
        self.nc = nc
        self.ops = {e: [] for e in self.ENGINES}
        self.cnt = {}
        self.sems = {}
        self.physcnt = {}
        self.released = set()
        self.virgin = set()
        self.dma_names = set()

    def sem(self, name):
        assert name not in self.released, f"sem {name} used after release"
        if name not in self.sems:
            h = self.nc.alloc_semaphore(name)
            base = self.physcnt.get(h.num, 0)
            self.sems[name] = (h, base)
            self.cnt[name] = 0
            self.virgin.add(name)
        return self.sems[name]

    def wait(self, eng, name, val):
        if val <= 0:
            return
        h, base = self.sem(name)
        self.ops[eng].append(lambda E, s=h, v=base + val: E.wait_ge(s, v))

    def wait_all(self, eng, name):
        self.wait(eng, name, self.cnt.get(name, 0))

    def inc_val(self, name, k=1):
        h, _ = self.sem(name)
        self.cnt[name] += k
        self.physcnt[h.num] = self.physcnt.get(h.num, 0) + k
        return self.cnt[name]

    def emit(self, eng, fn, inc=None, inck=1):
        if inc is not None:
            if inck == 16:
                self.dma_names.add(inc)
            h, base = self.sem(inc)
            if inc in self.virgin:
                self.virgin.discard(inc)
                if base > 0:
                    self.ops[eng].append(lambda E, s=h, v=base: E.wait_ge(s, v))
            self.inc_val(inc, inck)
            self.ops[eng].append(lambda E, fn=fn, s=h, k=inck: fn(E).then_inc(s, k))
        else:
            self.ops[eng].append(fn)

    def release(self, *prefixes):
        for name in list(self.sems):
            if any(name.startswith(pr) for pr in prefixes):
                h, _ = self.sems.pop(name)
                self.nc.release_semaphore(h)
                self.released.add(name)
                self.virgin.discard(name)

    nphase = 0

    def barrier_release(self, *prefixes):
        Prog.nphase += 1
        k = Prog.nphase
        names = [n for n in self.sems
                 if any(n.startswith(pr) for pr in prefixes)]
        for eng in self.ENGINES:
            for n in names:
                self.wait(eng, n, self.cnt.get(n, 0))
            if eng != "gpsimd":
                self.emit(eng, lambda E: E.nop(), inc=f"ack_{eng}")
        names = [n for n in names if n not in self.dma_names]
        for eng in self.ENGINES:
            if eng != "gpsimd":
                self.wait("gpsimd", f"ack_{eng}", k)
        handles = []
        for n in names:
            h, _ = self.sems.pop(n)
            handles.append(h)
            self.released.add(n)
            self.virgin.discard(n)
            self.physcnt[h.num] = 0
        for i, h in enumerate(handles):
            if i == len(handles) - 1:
                self.emit("gpsimd", lambda E, h=h: E.sem_clear(h), inc="phb")
            else:
                self.emit("gpsimd", lambda E, h=h: E.sem_clear(h))
        for h in handles:
            self.nc.release_semaphore(h)
        for eng in self.ENGINES:
            if eng != "gpsimd":
                self.wait(eng, "phb", k)

    def finalize(self):
        with self.nc.Block() as block:
            for eng in self.ENGINES:
                ops = self.ops[eng]
                if not ops:
                    continue
                getattr(block, eng)(lambda E, _ops=tuple(ops): [f(E) for f in _ops])


def build(core_id):
    nc = bass.Bass("TRN2", target_bir_lowering=False)
    p = Prog(nc)

    x_in = nc.dram_tensor("x", [NT, D], F32, kind="ExternalInput")
    rw_in = nc.dram_tensor("rw", [D, NE], F32, kind="ExternalInput")
    rb_in = nc.dram_tensor("rb", [1, NE], F32, kind="ExternalInput")
    w1_in = nc.dram_tensor("w1", [NE, D, 2 * FFN], F32, kind="ExternalInput")
    w2_in = nc.dram_tensor("w2", [NE, FFN, D], F32, kind="ExternalInput")
    wq_in = nc.dram_tensor("wq", [D, 3 * D], F32, kind="ExternalInput")
    phi_in = nc.dram_tensor("phi", [D, NP], F32, kind="ExternalInput")
    wf_in = nc.dram_tensor("wf", [64, D, D], F32, kind="ExternalInput")
    bf_in = nc.dram_tensor("bfb", [64, D], F32, kind="ExternalInput")
    y_out = nc.dram_tensor("y", [NT, D], F32, kind="ExternalOutput")

    kvt_send = nc.dram_tensor("kvt_send", [D, D + 8], F32)
    kvt_recv = nc.dram_tensor("kvt_recv", [D, D + 8], F32)
    e_send = nc.dram_tensor("e_send", [NT, NP], BF16)
    d_recv = nc.dram_tensor("d_recv", [NT, NP], BF16)
    s_send = nc.dram_tensor("s_send", [D, NP], BF16)
    s_recv = nc.dram_tensor("s_recv", [D, NP], BF16)
    wf_bf = nc.dram_tensor("wf_bf", [64, D, D], BF16)

    alloc = nc.alloc_sbuf_tensor
    x_fm = alloc("x_fm", [128, NDK * NT], F32)
    x_bf = alloc("x_bf", [128, NDK * NT], BF16)
    ident_f = alloc("ident_f", [128, 128], F32)
    ident_b = alloc("ident_b", [128, 128], BF16)
    onesf_r = alloc("onesf_r", [1, 128], F32)
    onesb_c = alloc("onesb_c", [128, 1], BF16)
    onesb_r = alloc("onesb_r", [1, 128], BF16)
    onesf_c = alloc("onesf_c", [128, 1], F32)
    cst = alloc("cst", [128, 4], F32)
    sel8 = alloc("sel8", [8, 8 * 128], BF16)
    pcol8 = alloc("pcol8", [8, 1], F32)
    sa_sb = [alloc(f"sa{i}", [128, 512], F32) for i in range(2)]
    tm_sb = [alloc(f"tmst{i}", [128, 512], F32) for i in range(2)]
    mbc_ = [alloc(f"mbc{i}", [128, 512], F32) for i in range(2)]
    att_kv = [alloc(f"akv{i}", [128, 512], BF16) for i in range(2)]
    kacc = alloc("kacc", [128, 1], F32)
    rw_bf = alloc("rw_bf", [128, NDK * NE], BF16)
    rb_row = alloc("rb_row", [1, NE], F32)
    rbb = alloc("rbb", [128, NE], F32)
    iota8 = alloc("iota8", [128, NE], F32)
    lg_tm = alloc("lg_tm", [128, 16 * NE], F32)
    mp_tm = alloc("mp_tm", [128, 16 * NE], F32)
    cols_a = alloc("cols_a", [128, 16], F32)
    cols_b = alloc("cols_b", [128, 8], F32)
    ks_col = alloc("ks_col", [128, 8], F32)
    ksb_col = alloc("ksb_col", [128, 8], BF16)
    bfl_bf = alloc("bfl_bf", [64, 1024], BF16)
    bo_bf = alloc("bo_bf", [128, 16 * 64], BF16)
    qmat = alloc("qmat", [128, 64], F32)
    boa = alloc("boa", [128, 64], F32)
    bob = alloc("bob", [128, 64], F32)
    iota64 = alloc("iota64", [128, 64], F32)
    cn_sb = alloc("cn_sb", [64, 512], BF16)

    ps_f = [nc.alloc_psum_tensor(f"ps{i}", [128, 512], F32) for i in range(6)]

    def TE(fn, inc=None):
        p.emit("tensor", fn, inc=inc)

    def VE(fn, inc=None):
        p.emit("vector", fn, inc=inc)

    def SC(fn, inc=None):
        p.emit("scalar", fn, inc=inc)

    def SY(fn, inc=None):
        p.emit("sync", fn, inc=inc, inck=16)

    def GP(fn, inc=None, inck=1):
        p.emit("gpsimd", fn, inc=inc, inck=inck)

    def GPD(fn, inc):
        p.emit("gpsimd", fn, inc=inc, inck=16)

    def mm_acc(bank_ap, ops, inc=None, tile_position=None):
        n = len(ops)
        for i, (l, r) in enumerate(ops):
            TE(lambda E, b=bank_ap, l=l, r=r, s=(i == 0), t=(i == n - 1),
               tp=tile_position: E.matmul(b, l, r, start=s, stop=t,
                                          tile_position=tp),
               inc=inc if i == n - 1 else None)

    def vsync(eng="vector"):
        p.emit(eng, lambda E: E.engine_nop(), inc=f"_vs_{eng}")
        p.wait(eng, f"_vs_{eng}", p.cnt[f"_vs_{eng}"])

    VE(lambda E: E.memset(ident_f[:, :], 0.0), inc="set0")
    VE(lambda E: E.memset(ident_b[:, :], 0.0), inc="set0")
    VE(lambda E: E.memset(onesf_r[:, :], 1.0), inc="set0")
    VE(lambda E: E.memset(onesb_c[:, :], 1.0), inc="set0")
    VE(lambda E: E.memset(onesf_c[:, :], 1.0), inc="set0")
    VE(lambda E: E.memset(onesb_r[:, :], 1.0), inc="set0")
    VE(lambda E: E.memset(cst[:, 0:1], LN_EPS), inc="set0")
    p.wait("gpsimd", "set0", 7)
    GP(lambda E: E.affine_select(
        out=ident_f[:, :], in_=ident_f[:, :], compare_op=ALU.not_equal,
        fill=1.0, base=0, pattern=[[-1, 128]], channel_multiplier=1), inc="setg")
    GP(lambda E: E.affine_select(
        out=ident_b[:, :], in_=ident_b[:, :], compare_op=ALU.not_equal,
        fill=1.0, base=0, pattern=[[-1, 128]], channel_multiplier=1), inc="setg")
    GP(lambda E: E.iota(iota8[:, :], [[1, NE]], channel_multiplier=0,
                        allow_small_or_imprecise_dtypes=True), inc="setg")
    GP(lambda E: E.iota(iota64[:, :], [[1, 64]], channel_multiplier=0,
                        allow_small_or_imprecise_dtypes=True), inc="setg")
    GP(lambda E: E.iota(pg32[:, 0:64] if False else qmat[:, :], [[-32, 64]],
                        channel_multiplier=1,
                        allow_small_or_imprecise_dtypes=True), inc="setg")
    GP(lambda E: E.iota(pcol8[:, :], [[0, 1]], channel_multiplier=1,
                        allow_small_or_imprecise_dtypes=True), inc="setg")
    selt_scope = nc.sbuf_tensor("selt", [8, 8 * 128], F32)
    selt = selt_scope.__enter__()
    GP(lambda E: E.iota(selt[:, :].rearrange("p (e x) -> p e x", e=8),
                        [[1, 8], [0, 128]], channel_multiplier=0,
                        allow_small_or_imprecise_dtypes=True), inc="setg")
    p.wait("vector", "setg", 7)
    VE(lambda E: E.tensor_scalar(sel8[:, :], selt[:, :], pcol8[:, 0:1],
                                 None, ALU.is_equal), inc="selg")
    selt_scope.__exit__(None, None, None)
    for ch in range(16):
        VE(lambda E, ch=ch: E.tensor_single_scalar(
            boa[:, :], qmat[:, :], float(-128 * ch), ALU.is_ge), inc="bo1")
        p.wait("vector", "bo1", p.cnt["bo1"])
        VE(lambda E, ch=ch: E.tensor_single_scalar(
            bob[:, :], qmat[:, :], float(32 - 128 * ch), ALU.is_lt), inc="bo1")
        p.wait("vector", "bo1", p.cnt["bo1"])
        VE(lambda E, ch=ch: E.tensor_mul(
            bo_bf[:, ch * 64:(ch + 1) * 64], boa[:, :], bob[:, :]), inc="bo2")

    GPD(lambda E: E.dma_start(
        out=rw_bf[:, :].rearrange("p (k e) -> p k e", k=NDK),
        in_=rw_in[:, :].rearrange("(k p) e -> p k e", p=128)), inc="rwl")
    GPD(lambda E: E.dma_start(out=rb_row[:, :], in_=rb_in[:, :]), inc="rbl")
    GPD(lambda E: E.dma_start(out=bfl_bf[:, :], in_=bf_in[:, :]), inc="bfll")

    ntr = 0
    with contextlib.ExitStack() as _st:
      stgi0 = _st.enter_context(nc.sbuf_tensor("stgi0", [128, 1024], F32))
      stgi1 = _st.enter_context(nc.sbuf_tensor("stgi1", [128, 1024], F32))
      stg = [stgi0, stgi1]
      for nm in ("selg", "setg", "set0", "pgs", "setg2"):
          p.wait("sync", nm, p.cnt.get(nm, 0))
      for mt in range(16):
        b = mt % 2
        if mt >= 2:
            p.wait("sync", "xevac", (mt - 1) * 8)
        SY(lambda E, b=b, mt=mt, stg=stg: E.dma_start(
            out=stg[b][:, :], in_=x_in[mt * 128:(mt + 1) * 128, :]), inc=f"xld{b}")
        p.wait("tensor", f"xld{b}", (mt // 2 + 1) * 16)
        if mt == 0:
            p.wait("tensor", "setg", 2)
        for dk in range(NDK):
            pb = ntr % 2
            p.wait("tensor", "xevac", ntr - 1)
            TE(lambda E, b=b, dk=dk, pb=pb, stg=stg: E.transpose(
                ps_f[pb][:, 0:128], stg[b][:, dk * 128:(dk + 1) * 128],
                ident_f[:, :]), inc="xtr")
            p.wait("vector", "xtr", ntr + 1)
            col = dk * NT + mt * 128
            VE(lambda E, pb=pb, col=col: E.tensor_copy(
                x_fm[:, col:col + 128], ps_f[pb][:, 0:128]))
            VE(lambda E, pb=pb, col=col: E.tensor_copy(
                x_bf[:, col:col + 128], ps_f[pb][:, 0:128]), inc="xevac")
            ntr += 1

    def layer_norm(tag):
        sq, st, ev, ro, rs, bc, ap, cz = (f"{tag}{s}" for s in
                                          ("sq", "st", "ev", "ro", "rs", "bc", "ap", "cz"))
        with contextlib.ExitStack() as _st:
            rows_a = _st.enter_context(nc.sbuf_tensor(f"{tag}rowa", [1, NT], F32))
            rows_b = _st.enter_context(nc.sbuf_tensor(f"{tag}rowb", [1, NT], F32))
            rows_c = _st.enter_context(nc.sbuf_tensor(f"{tag}rowc", [1, NT], F32))
            ln_sq = _st.enter_context(nc.sbuf_tensor(f"{tag}sqb", [128, NDK * 512], BF16))
            _layer_norm_body(tag, sq, st, ev, ro, rs, bc, ap, cz,
                             rows_a, rows_b, rows_c, ln_sq)

    def _layer_norm_body(tag, sq, st, ev, ro, rs, bc, ap, cz,
                         rows_a, rows_b, rows_c, ln_sq):
        for tg in range(4):
            for dj in range(NDK):
                col = dj * NT + tg * 512
                VE(lambda E, col=col: E.tensor_copy(
                    x_bf[:, col:col + 512], x_fm[:, col:col + 512]), inc=cz)
            p.wait("scalar", cz, p.cnt[cz])
            if tg >= 1:
                p.wait("scalar", st, 2 * tg)
            for dj in range(NDK):
                col = dj * NT + tg * 512
                SC(lambda E, dj=dj, col=col: E.activation(
                    ln_sq[:, dj * 512:(dj + 1) * 512], x_bf[:, col:col + 512],
                    AF.Square), inc=sq)
            p.wait("tensor", cz, p.cnt[cz])
            p.wait("tensor", sq, p.cnt[sq])
            if tg >= 1:
                p.wait("tensor", ev, 2 * tg)
            mm_acc(ps_f[4][0:1, :],
                   [(onesb_c[:, :], x_bf[:, dj * NT + tg * 512:dj * NT + tg * 512 + 512])
                    for dj in range(NDK)], inc=st)
            mm_acc(ps_f[5][0:1, :],
                   [(onesb_c[:, :], ln_sq[:, dj * 512:(dj + 1) * 512])
                    for dj in range(NDK)], inc=st)
            p.wait("vector", st, p.cnt[st])
            c0 = tg * 512
            VE(lambda E, c0=c0: E.tensor_scalar_mul(
                rows_a[:, c0:c0 + 512], ps_f[4][0:1, :], 1.0 / D), inc=ev)
            VE(lambda E, c0=c0: E.tensor_scalar_mul(
                rows_b[:, c0:c0 + 512], ps_f[5][0:1, :], 1.0 / D), inc=ev)
        p.wait("vector", ev, p.cnt[ev])
        VE(lambda E: E.tensor_mul(rows_c[:, :], rows_a[:, :], rows_a[:, :]), inc=ro)
        p.wait("vector", ro, p.cnt[ro])
        VE(lambda E: E.tensor_sub(rows_b[:, :], rows_b[:, :], rows_c[:, :]), inc=ro)
        p.wait("scalar", ro, p.cnt[ro])
        p.wait("scalar", "set0", 6)
        SC(lambda E: E.activation(rows_c[:, :], rows_b[:, :], AF.Ln,
                                  bias=cst[0:1, 0:1]), inc=rs)
        p.wait("scalar", rs, p.cnt[rs])
        SC(lambda E: E.activation(rows_b[:, :], rows_c[:, :], AF.Exp,
                                  scale=-0.5), inc=rs)
        for tg in range(4):
            bmean = ps_f[(tg % 2) * 2 + 0]
            brstd = ps_f[(tg % 2) * 2 + 1]
            p.wait("tensor", rs, p.cnt[rs])
            if tg >= 2:
                p.wait("tensor", ap, (tg - 1) * 16)
            c0 = tg * 512
            TE(lambda E, b=bmean, c0=c0: E.matmul(
                b[:, :], onesf_r[:, :], rows_a[0:1, c0:c0 + 512],
                start=True, stop=True), inc=bc)
            TE(lambda E, b=brstd, c0=c0: E.matmul(
                b[:, :], onesf_r[:, :], rows_b[0:1, c0:c0 + 512],
                start=True, stop=True), inc=bc)
            p.wait("vector", bc, p.cnt[bc])
            for dj in range(NDK):
                col = dj * NT + tg * 512
                sb = dj % 2
                VE(lambda E, col=col, b=bmean, sb=sb: E.tensor_sub(
                    tm_sb[sb][:, :], x_fm[:, col:col + 512], b[:, :]))
                vsync()
                VE(lambda E, col=col, b=brstd, sb=sb: E.tensor_mul(
                    x_fm[:, col:col + 512], tm_sb[sb][:, :], b[:, :]), inc=ap)
                vsync()
                VE(lambda E, col=col: E.tensor_copy(
                    x_bf[:, col:col + 512], x_fm[:, col:col + 512]), inc=ap)
        p.wait("tensor", ap, p.cnt[ap])


    def phase_barrier(tag, dma_sems=()):
        for eng in ("gpsimd", "sync", "scalar", "vector", "tensor"):
            p.wait(eng, f"{tag}ap", p.cnt[f"{tag}ap"])
            for ds in dma_sems:
                p.wait(eng, ds, p.cnt.get(ds, 0))

    if PHASES >= 1:
        with contextlib.ExitStack() as _st:
            pb_t0 = _st.enter_context(nc.psum_tensor("pb_t0", [128, 512], BF16))
            pb_t1 = _st.enter_context(nc.psum_tensor("pb_t1", [128, 512], BF16))
            xev = _st.enter_context(nc.sbuf_tensor("xev", [128, 16 * 512], BF16))
            ao_fm = _st.enter_context(nc.sbuf_tensor("ao_fm", [128, 16 * 512], F32))
            att_sb = _st.enter_context(nc.sbuf_tensor("att_sb", [128, 512], BF16))
            attT0 = _st.enter_context(nc.sbuf_tensor("attT0", [128, 512], BF16))
            attT1 = _st.enter_context(nc.sbuf_tensor("attT1", [128, 512], BF16))
            xs_tm = _st.enter_context(nc.sbuf_tensor("xs_tm", [128, 256], BF16))
            arow = _st.enter_context(nc.sbuf_tensor("arow", [1, 512], F32))
            nmax = _st.enter_context(nc.sbuf_tensor("nmax", [128, 2], F32))
            rsum = _st.enter_context(nc.sbuf_tensor("rsum", [128, 2], F32))
            pb_t = [pb_t0, pb_t1]
            p.wait("vector", "xevac", 128)
            for dj in range(NDK):
                for si in range(2):
                    src0 = dj * NT + si * 1024
                    dst0 = (dj * 2 + si) * 512
                    VE(lambda E, src0=src0, dst0=dst0: E.tensor_copy(
                        xev[:, dst0:dst0 + 512], x_fm[:, src0:src0 + 1024:2]),
                       inc="p1ev")
            nsh = 0
            for si in range(2):
                for hh in range(16):
                    dj = hh // 2
                    hp = (hh % 2) * 64
                    hcol = (dj * 2 + si) * 512
                    tp_out = (0, hp) if hp else None
                    if nsh == 0:
                        p.wait("tensor", "p1ev", 16)
                        p.wait("tensor", "xevac", 128)
                    p.wait("tensor", "p1xs", nsh)
                    for j in range(4):
                        TE(lambda E, hp=hp, hcol=hcol, j=j: E.transpose(
                            pb_t0[:, j * 64:(j + 1) * 64],
                            xev[hp:hp + 64, hcol + j * 128:hcol + (j + 1) * 128],
                            ident_b[hp:hp + 64, hp:hp + 64]),
                           inc="p1tx" if j == 3 else None)
                    p.wait("vector", "p1tx", p.cnt["p1tx"])
                    VE(lambda E: E.tensor_copy(xs_tm[:, 0:256], pb_t0[:, 0:256]),
                       inc="p1xs")
                    for qs in range(4):
                        k = nsh * 4 + qs
                        sbank = ps_f[k % 2]
                        c = k % 2
                        p.wait("tensor", "p1at", k - 1)
                        TE(lambda E, hp=hp, hcol=hcol, qs=qs, b=sbank: E.matmul(
                            b[:, :],
                            xev[hp:hp + 64, hcol + qs * 128:hcol + qs * 128 + 128],
                            xev[hp:hp + 64, hcol:hcol + 512],
                            start=True, stop=True), inc="p1sc")
                        p.wait("vector", "p1sc", k + 1)
                        VE(lambda E, b=sbank, c=c: E.reduce_max(
                            nmax[:, c:c + 1], b[:, :], axis=AX.X, negate=True),
                           inc="p1mx")
                        p.wait("vector", "p1mx", p.cnt["p1mx"])
                        VE(lambda E, c=c: E.tensor_scalar_mul(
                            nmax[:, c:c + 1], nmax[:, c:c + 1], 0.125), inc="p1m2")
                        p.wait("scalar", "p1m2", p.cnt["p1m2"])
                        p.wait("scalar", "p1at", k - 1)
                        SC(lambda E, b=sbank, c=c: E.activation(
                            sa_sb[c][:, :], b[:, :], AF.Exp, scale=0.125,
                            bias=nmax[:, c:c + 1], accum_out=rsum[:, c:c + 1]),
                           inc="p1ex")
                        p.wait("vector", "p1ex", p.cnt["p1ex"])
                        VE(lambda E, c=c: E.reciprocal(
                            rsum[:, c:c + 1], rsum[:, c:c + 1]), inc="p1rc")
                        p.wait("vector", "p1rc", p.cnt["p1rc"])
                        p.wait("vector", "p1tt", p.cnt.get("p1tt", 0))
                        VE(lambda E, c=c: E.tensor_scalar_mul(
                            att_sb[:, :], sa_sb[c][:, :], rsum[:, c:c + 1]),
                           inc="p1at")
                        p.wait("tensor", "p1at", p.cnt["p1at"])
                        p.wait("tensor", "p1te", k - 1)
                        for j in range(4):
                            TE(lambda E, j=j: E.transpose(
                                pb_t1[:, j * 128:(j + 1) * 128],
                                att_sb[:, j * 128:(j + 1) * 128], ident_b[:, :]),
                               inc="p1tt" if j == 3 else None)
                        attT = [attT0, attT1][k % 2]
                        p.wait("vector", "p1tt", p.cnt["p1tt"])
                        p.wait("vector", "p1ou", k - 1)
                        VE(lambda E, attT=attT: E.tensor_copy(
                            attT[:, :], pb_t1[:, :]), inc="p1te")
                        obank = ps_f[2 + (nsh % 2)]
                        p.wait("tensor", "p1te", k + 1)
                        if qs == 0:
                            p.wait("tensor", "p1oe", nsh - 1)
                        for j in range(4):
                            TE(lambda E, j=j, o=obank, hp=hp, qs=qs, tp=tp_out,
                               attT=attT:
                               E.matmul(o[hp:hp + 64, qs * 128:qs * 128 + 128],
                                        xs_tm[:, j * 64:(j + 1) * 64],
                                        attT[:, j * 128:(j + 1) * 128],
                                        start=(j == 0), stop=(j == 3),
                                        tile_position=tp), inc="p1ou" if j == 3 else None)
                    p.wait("vector", "p1ou", p.cnt["p1ou"])
                    VE(lambda E, hp=hp, o=ps_f[2 + (nsh % 2)], dj=dj, si=si:
                       E.tensor_copy(
                           ao_fm[hp:hp + 64, (dj * 2 + si) * 512:(dj * 2 + si) * 512 + 512],
                           o[hp:hp + 64, :]), inc="p1oe")
                    nsh += 1
            for si in range(2):
                p.wait("tensor", "p1oe", 32)
                p.wait("tensor", "p1rr", p.cnt.get("p1rr", 0))
                mm_acc(ps_f[4][0:1, :],
                       [(onesf_c[:, :], ao_fm[:, (dj * 2 + si) * 512:(dj * 2 + si) * 512 + 512])
                        for dj in range(NDK)], inc="p1dn")
                p.wait("vector", "p1dn", p.cnt["p1dn"])
                VE(lambda E: E.tensor_copy(arow[:, :], ps_f[4][0:1, :]), inc="p1al")
                vsync()
                VE(lambda E: E.reciprocal(arow[:, :], arow[:, :]), inc="p1al")
                vsync()
                VE(lambda E: E.tensor_mul(arow[:, :], ps_f[4][0:1, :], arow[:, :]),
                   inc="p1al")
                p.wait("tensor", "p1al", p.cnt["p1al"])
                TE(lambda E: E.matmul(ps_f[5][:, :], onesf_r[:, :], arow[0:1, :],
                                      start=True, stop=True), inc="p1ab")
                p.wait("vector", "p1ab", p.cnt["p1ab"])
                for dj in range(NDK):
                    sb = dj % 2
                    acol = (dj * 2 + si) * 512
                    xcol = dj * NT + si * 1024
                    VE(lambda E, acol=acol, sb=sb: E.tensor_mul(
                        tm_sb[sb][:, :], ao_fm[:, acol:acol + 512], ps_f[5][:, :]))
                    vsync()
                    VE(lambda E, xcol=xcol, sb=sb: E.tensor_add(
                        x_fm[:, xcol:xcol + 1024:2], x_fm[:, xcol:xcol + 1024:2],
                        tm_sb[sb][:, :]), inc="p1rr")
                p.wait("tensor", "p1rr", p.cnt["p1rr"])
        layer_norm("n1")
        p.barrier_release("p1", "n1", "xld", "xtr", "xevac")

    if PHASES >= 2:
        with contextlib.ExitStack() as _st:
            psy0 = _st.enter_context(nc.psum_tensor("psy0", [128, 512], F32))
            psy1 = _st.enter_context(nc.psum_tensor("psy1", [128, 512], F32))
            u_buf = _st.enter_context(nc.sbuf_tensor("u_buf", [128, 32 * SG], BF16))
            w1a0 = _st.enter_context(nc.sbuf_tensor("w1a0", [128, 2048], BF16))
            w1a1 = _st.enter_context(nc.sbuf_tensor("w1a1", [128, 2048], BF16))
            w1g0 = _st.enter_context(nc.sbuf_tensor("w1g0", [128, 2048], BF16))
            w1g1 = _st.enter_context(nc.sbuf_tensor("w1g1", [128, 2048], BF16))
            w2b0 = _st.enter_context(nc.sbuf_tensor("w2b0", [128, 512], BF16))
            w2b1 = _st.enter_context(nc.sbuf_tensor("w2b1", [128, 512], BF16))
            m_fm = _st.enter_context(nc.sbuf_tensor("m_fm", [8, NT], BF16))
            et8 = _st.enter_context(nc.sbuf_tensor("et8", [128, 8], F32))
            z8 = _st.enter_context(nc.sbuf_tensor("z8", [128, 8], F32))
            m8 = _st.enter_context(nc.sbuf_tensor("m8", [128, 8], F32))
            idx8 = _st.enter_context(nc.sbuf_tensor("idx8", [128, 8], mybir.dt.uint32))
            mcol = _st.enter_context(nc.sbuf_tensor("mcol", [128, 4], F32))
            w1a = [w1a0, w1a1]
            w1g = [w1g0, w1g1]
            w2b = [w2b0, w2b1]
            ps8 = ps_f + [psy0, psy1]
            p.wait("tensor", "rwl", 16)
            for mt in range(16):
                bank = ps_f[4] if mt % 2 == 0 else ps_f[5]
                if mt >= 2:
                    p.wait("tensor", "p2lg", mt - 1)
                mm_acc(bank[:, 0:8],
                       [(x_bf[:, dk * NT + mt * 128:dk * NT + mt * 128 + 128],
                         rw_bf[:, dk * 8:(dk + 1) * 8]) for dk in range(NDK)],
                       inc="p2lgm")
                p.wait("vector", "p2lgm", p.cnt["p2lgm"])
                VE(lambda E, mt=mt, b=bank: E.tensor_copy(
                    lg_tm[:, mt * 8:(mt + 1) * 8], b[:, 0:8]), inc="p2lg")
            p.wait("tensor", "rbl", 16)
            p.wait("tensor", "p2lg", 16)
            TE(lambda E: E.matmul(ps_f[4][:, 0:8], onesf_r[:, :], rb_row[:, :],
                                  start=True, stop=True), inc="p2rb")
            p.wait("vector", "p2rb", 1)
            VE(lambda E: E.tensor_copy(rbb[:, :], ps_f[4][:, 0:8]), inc="p2rbe")
            vsync()
            for mt in range(16):
                VE(lambda E, mt=mt: E.tensor_add(
                    lg_tm[:, mt * 8:(mt + 1) * 8], lg_tm[:, mt * 8:(mt + 1) * 8],
                    rbb[:, :]), inc="p2lb")
            VE(lambda E: E.memset(z8[:, :], 0.0), inc="p2z8")
            p.wait("vector", "p2z8", 1)
            for mt in range(16):
                p.wait("vector", "p2lb", 16)
                VE(lambda E, mt=mt: E.reduce_max(
                    mcol[:, 0:1], lg_tm[:, mt * 8:(mt + 1) * 8], axis=AX.X),
                   inc="p2w")
                vsync()
                VE(lambda E: E.tensor_scalar_mul(mcol[:, 1:2], mcol[:, 0:1], -1.0),
                   inc="p2w")
                p.wait("scalar", "p2w", p.cnt["p2w"])
                SC(lambda E, mt=mt: E.activation(
                    et8[:, 0:8], lg_tm[:, mt * 8:(mt + 1) * 8], AF.Exp,
                    bias=mcol[:, 1:2], accum_out=mcol[:, 2:3]), inc="p2e")
                p.wait("vector", "p2e", p.cnt["p2e"])
                VE(lambda E: E.reciprocal(mcol[:, 2:3], mcol[:, 2:3]), inc="p2w")
                vsync()
                VE(lambda E: E.tensor_scalar_add(m8[:, :], z8[:, :],
                                                 mcol[:, 0:1]), inc="p2w")
                vsync()
                VE(lambda E, mt=mt: E.max_index(
                    idx8[:, :], m8[:, :], lg_tm[:, mt * 8:(mt + 1) * 8]),
                   inc="p2w")
                vsync()
                VE(lambda E: E.tensor_copy(mcol[:, 3:4], idx8[:, 0:1]), inc="p2w")
                vsync()
                VE(lambda E, mt=mt: E.tensor_scalar(
                    mp_tm[:, mt * 8:(mt + 1) * 8], iota8[:, :], mcol[:, 3:4],
                    None, ALU.is_equal), inc="p2w")
                vsync()
                VE(lambda E, mt=mt: E.tensor_scalar(
                    mp_tm[:, mt * 8:(mt + 1) * 8], mp_tm[:, mt * 8:(mt + 1) * 8],
                    mcol[:, 2:3], None, ALU.mult), inc="p2mp")
            for mt in range(16):
                p.wait("tensor", "p2mp", 16)
                p.wait("tensor", "p2mf", mt)
                TE(lambda E, mt=mt: E.transpose(
                    ps_f[4][0:8, 0:128], mp_tm[:, mt * 8:(mt + 1) * 8],
                    ident_f[:, :]), inc="p2mt")
                p.wait("vector", "p2mt", p.cnt["p2mt"])
                VE(lambda E, mt=mt: E.tensor_copy(
                    m_fm[0:8, mt * 128:(mt + 1) * 128], ps_f[4][0:8, 0:128]),
                   inc="p2mf")

            ia = 0
            nw1 = 0
            nw2 = 0
            mb = 0
            npass = 0
            nsg_done = 0
            nwf = 0
            w2_marks = []

            def w1_load(e, fc):
                nonlocal nw1, nwf
                par = nw1 % 2
                if nw1 >= 2:
                    kprev = nw1 - 2
                    p.wait("gpsimd", "agrp", (kprev + 1) * 4)
                    p.wait("gpsimd", "ggrp", (kprev + 1) * 4)
                for half, buf in ((0, w1a), (1, w1g)):
                    base = half * FFN + fc * 256
                    src = w1_in[e, :, base:base + 256].rearrange("(k p) f -> p k f", p=128)
                    dst = buf[par][:, :].rearrange("p (k f) -> p k f", k=NDK)
                    GPD(lambda E, src=src, dst=dst: E.dma_start(out=dst, in_=src),
                        inc=f"w1l{par}")
                nw1 += 1
                if PHASES >= 4 and nw1 % 4 == 0 and nwf < 64:
                    n = nwf
                    GPD(lambda E, n=n: E.dma_start(
                        out=wf_bf[n, :, :], in_=wf_in[n, :, :]), inc="wfc")
                    nwf += 1

            def w2_load(e, fi, rnd):
                nonlocal nw2
                par = nw2 % 2
                if nw2 >= 2:
                    sname, sval = w2_marks[nw2 - 2]
                    p.wait("gpsimd", sname, sval)
                src = w2_in[e, fi * 128:(fi + 1) * 128, rnd * 512:(rnd + 1) * 512]
                GPD(lambda E, src=src, par=par: E.dma_start(
                    out=w2b[par][:, 0:512], in_=src), inc=f"w2l{par}")
                nw2 += 1

            blocks = [(e, sg, fc) for e in range(NE) for sg in range(NSG)
                      for fc in range(NFC)]
            w1_load(0, 0)
            w1_load(0, 1)

            for e in range(NE):
                for sg in range(NSG):
                    gsg = e * NSG + sg
                    for fc in range(NFC):
                        bi = gsg * NFC + fc
                        if bi + 2 < len(blocks):
                            w1_load(blocks[bi + 2][0], blocks[bi + 2][2])
                        for sub in range(2):
                            j = ia // 2
                            par = bi % 2
                            bpar = (fc * 2 + sub) % 2
                            p.wait("tensor", f"w1l{par}", (bi // 2 + 1) * 32)
                            p.wait("tensor", "p2sil", ia - 2)
                            p.wait("tensor", "p2um", ia - 2)
                            if fc == 0 and sub == 0:
                                p.wait("tensor", "p2mbk", mb)
                                p.wait("tensor", "p2ym", npass * 8)
                            for half in range(2):
                                wb = w1a if half == 0 else w1g
                                grp = "agrp" if half == 0 else "ggrp"
                                for dk in range(NDK):
                                    lcol = dk * 256 + sub * 128
                                    for th in range(2):
                                        tgt = ps8[half * 4 + th * 2 + bpar]
                                        rcol = dk * NT + sg * SG + th * 512
                                        TE(lambda E, tgt=tgt, wb=wb, lcol=lcol,
                                           rcol=rcol, par=par, s=(dk == 0),
                                           t=(dk == 7): E.matmul(
                                            tgt[:, :], wb[par][:, lcol:lcol + 128],
                                            x_bf[:, rcol:rcol + 512],
                                            start=s, stop=t),
                                           inc=grp if (dk == 7) else None)
                            for th in range(2):
                                iat = ia + th
                                abk = half_bank = th * 2 + bpar
                                p.wait("scalar", "agrp", iat + 1)
                                p.wait("scalar", "p2um", iat - 1)
                                SC(lambda E, bk=ps8[th * 2 + bpar], sl=iat % 2:
                                   E.activation(sa_sb[sl][:, :], bk[:, :], AF.Silu),
                                   inc="p2sil")
                                fi = fc * 2 + sub
                                ucol = fi * SG + th * 512
                                p.wait("vector", "p2sil", iat + 1)
                                p.wait("vector", "ggrp", iat + 1)
                                if iat % 64 == 0 and gsg > 0:
                                    p.wait("vector", "p2yb", nsg_done * 2)
                                VE(lambda E, sl=iat % 2, ucol=ucol,
                                   gk=ps8[4 + th * 2 + bpar]: E.tensor_mul(
                                    u_buf[:, ucol:ucol + 512], sa_sb[sl][:, :],
                                    gk[:, :]), inc="p2um")
                            ia += 2
                    p.wait("tensor", "p2um", ia)
                    p.wait("tensor", "p2sil", ia)
                    for th in range(2):
                        c0 = sg * SG + th * 512
                        p.wait("tensor", "p2mbk", mb)
                        if e == 0 and sg == 0 and th == 0:
                            p.wait("tensor", "selg", 1)
                        TE(lambda E, c0=c0, e=e: E.matmul(
                            ps_f[0][:, :], sel8[:, e * 128:(e + 1) * 128],
                            m_fm[0:8, c0:c0 + 512],
                            start=True, stop=True), inc="p2mbc")
                        p.wait("vector", "p2mbc", mb + 1)
                        VE(lambda E, b=mb % 2: E.tensor_copy(
                            mbc_[b][:, :], ps_f[0][:, :]), inc="p2mbk")
                        mb += 1
                    mbth = {0: (mb - 2) % 2, 1: (mb - 1) % 2}
                    p.wait("tensor", "p2mbk", mb)
                    for rnd in range(2):
                        for fi in range(32):
                            w2_load(e, fi, rnd)
                            p.wait("tensor", f"w2l{(nw2 - 1) % 2}",
                                   ((nw2 - 1) // 2 + 1) * 16)
                            for dcq in range(4):
                                for th in range(2):
                                    bank = dcq * 2 + th
                                    if fi == 0:
                                        p.wait("tensor", "p2ym",
                                               (npass - 1) * 8 + bank + 1)
                                    ucol = fi * SG + th * 512
                                    wcol = dcq * 128
                                    last = (fi == 31 and bank == 7)
                                    inc = ("p2yb" if last else
                                           ("p2wu" if bank == 7 else None))
                                    TE(lambda E, bank=bank, wcol=wcol, ucol=ucol,
                                       par=(nw2 - 1) % 2, s=(fi == 0),
                                       t=(fi == 31): E.matmul(
                                        ps8[bank][:, :], w2b[par][:, wcol:wcol + 128],
                                        u_buf[:, ucol:ucol + 512], start=s, stop=t),
                                       inc=inc)
                                    if bank == 7:
                                        w2_marks.append(
                                            ("p2yb", p.cnt["p2yb"]) if last
                                            else ("p2wu", p.cnt["p2wu"]))
                        p.wait("vector", "p2yb", npass + 1)
                        for dcq in range(4):
                            for th in range(2):
                                bank = dcq * 2 + th
                                dc = rnd * 4 + dcq
                                xcol = dc * NT + sg * SG + th * 512
                                sb = bank % 2
                                VE(lambda E, bank=bank, sb=sb, mbb=mbth[th]:
                                   E.tensor_mul(tm_sb[sb][:, :], ps8[bank][:, :],
                                                mbc_[mbb][:, :]), inc="p2ym")
                                vsync()
                                VE(lambda E, sb=sb, xcol=xcol: E.tensor_add(
                                    x_fm[:, xcol:xcol + 512],
                                    x_fm[:, xcol:xcol + 512], tm_sb[sb][:, :]),
                                   inc="p2ya")
                        npass += 1
                    nsg_done += 1
            p.wait("tensor", "p2ya", p.cnt["p2ya"])
        layer_norm("n2")
        p.barrier_release("p2", "n2", "w1l", "w2l", "agrp", "ggrp", "rwl", "rbl")

    if PHASES >= 3:
        with contextlib.ExitStack() as _st:
            pb_t0 = _st.enter_context(nc.psum_tensor("pb_t0a", [128, 512], BF16))
            pb_t1 = _st.enter_context(nc.psum_tensor("pb_t1a", [128, 512], BF16))
            k_tm = _st.enter_context(nc.sbuf_tensor("k_tm", [128, 16 * 1024], BF16))
            v_tm = _st.enter_context(nc.sbuf_tensor("v_tm", [128, 16 * 1024], BF16))
            wqb0 = _st.enter_context(nc.sbuf_tensor("wqb0", [128, 4096], BF16))
            wqb1 = _st.enter_context(nc.sbuf_tensor("wqb1", [128, 4096], BF16))
            wsb = [wqb0, wqb1]
            pb_t = [pb_t0, pb_t1]
            VE(lambda E: E.memset(ks_col[:, :], 0.0), inc="p3ks")
            nblk = 0
            nkv = 0
            for part, colbase in (("k", D), ("v", 2 * D)):
                for cg in range(2):
                    par = nblk % 2
                    if nblk >= 2:
                        p.wait("gpsimd", "p3kv", (nblk - 1) * 16)
                    src = wq_in[:, colbase + cg * 512:colbase + (cg + 1) * 512]
                    GPD(lambda E, src=src, buf=wsb[par]: E.dma_start(
                        out=buf[:, :].rearrange("p (k f) -> p k f", k=NDK),
                        in_=src.rearrange("(k p) f -> p k f", p=128)),
                        inc=f"p3w{par}")
                    nblk += 1
                    for cs in range(4):
                        for tg in range(4):
                            bank = ps_f[nkv % 2]
                            sb = nkv % 2
                            p.wait("tensor", f"p3w{par}",
                                   ((nblk - 1) // 2 + 1) * 16)
                            p.wait("tensor", "p3ke", nkv - 1)
                            mm_acc(bank[:, :],
                                   [(wsb[par][:, dk * 512 + cs * 128:dk * 512 + cs * 128 + 128],
                                     x_bf[:, dk * NT + tg * 512:dk * NT + tg * 512 + 512])
                                    for dk in range(NDK)], inc="p3kv")
                            kc = cg * 4 + cs
                            p.wait("vector", "p3kv", p.cnt["p3kv"])
                            p.wait("vector", "p3tt", nkv - 1)
                            if part == "k":
                                VE(lambda E, b=bank, sb=sb: E.tensor_scalar_min(
                                    sa_sb[sb][:, :], b[:, :], 0.0), inc="p3mn")
                                p.wait("scalar", "p3mn", p.cnt["p3mn"])
                                p.wait("scalar", "p3ke", nkv - 1)
                                SC(lambda E, sb=sb: E.activation(
                                    tm_sb[sb][:, :], sa_sb[sb][:, :], AF.Exp),
                                   inc="p3ex")
                                p.wait("vector", "p3ex", p.cnt["p3ex"])
                                VE(lambda E, b=bank, sb=sb: E.tensor_scalar_max(
                                    sa_sb[sb][:, :], b[:, :], 0.0), inc="p3mx")
                                vsync()
                                VE(lambda E, sb=sb: E.scalar_tensor_tensor(
                                    att_kv[sb][:, :], tm_sb[sb][:, :], 0.0,
                                    sa_sb[sb][:, :], ALU.add, ALU.add,
                                    accum_out=kacc[:, 0:1]), inc="p3ke")
                                vsync()
                                VE(lambda E, kc=kc: E.tensor_add(
                                    ks_col[:, kc:kc + 1], ks_col[:, kc:kc + 1],
                                    kacc[:, 0:1]), inc="p3ks")
                            else:
                                VE(lambda E, b=bank, sb=sb: E.tensor_copy(
                                    att_kv[sb][:, :], b[:, :]), inc="p3ke")
                            dst = k_tm if part == "k" else v_tm
                            p.wait("tensor", "p3ke", p.cnt["p3ke"])
                            p.wait("tensor", "p3te", (nkv - 1) * 4)
                            for j in range(4):
                                TE(lambda E, j=j, b=pb_t[sb], sb=sb:
                                   E.transpose(b[:, j * 128:(j + 1) * 128],
                                               att_kv[sb][:, j * 128:(j + 1) * 128],
                                               ident_b[:, :]),
                                   inc="p3tt" if j == 3 else None)
                            p.wait("vector", "p3tt", p.cnt["p3tt"])
                            for j in range(4):
                                tcol = (tg * 4 + j) * 1024 + kc * 128
                                VE(lambda E, j=j, b=pb_t[sb], tcol=tcol, dst=dst:
                                   E.tensor_copy(dst[:, tcol:tcol + 128],
                                                 b[:, j * 128:(j + 1) * 128]),
                                   inc="p3te")
                            nkv += 1
            nkt = 0
            p.wait("tensor", "p3te", p.cnt["p3te"])
            for dc in range(8):
                for hg in range(2):
                    bank = ps_f[2 + (nkt % 2)]
                    sb = nkt % 2
                    p.wait("tensor", "p3kte", nkt - 1)
                    mm_acc(bank[:, :],
                           [(k_tm[:, lc * 1024 + dc * 128:lc * 1024 + dc * 128 + 128],
                             v_tm[:, lc * 1024 + hg * 512:lc * 1024 + hg * 512 + 512])
                            for lc in range(16)], inc="p3kt")
                    p.wait("vector", "p3kt", p.cnt["p3kt"])
                    if nkt >= 2:
                        p.wait("vector", f"p3sd{sb}", ((nkt - 2) // 2 + 1) * 16)
                    VE(lambda E, b=bank, sb=sb: E.tensor_copy(
                        tm_sb[sb][:, :], b[:, :]), inc="p3kte")
                    p.wait("sync", "p3kte", p.cnt["p3kte"])
                    SY(lambda E, dc=dc, hg=hg, sb=sb: E.dma_start(
                        out=kvt_send[dc * 128:(dc + 1) * 128, hg * 512:(hg + 1) * 512],
                        in_=tm_sb[sb][:, :]), inc=f"p3sd{sb}")
                    nkt += 1
            p.wait("sync", "p3ks", p.cnt["p3ks"])
            for cc in range(8):
                SY(lambda E, cc=cc: E.dma_start(
                    out=kvt_send[:, D + cc:D + cc + 1].rearrange(
                        "(k p) c -> p k c", p=128),
                    in_=ks_col[:, :].rearrange("p (k c) -> p k c", c=1),
                    allow_slow_non_contiguous=True),
                   inc="p3sd0")
            p.wait("gpsimd", "p3sd0", p.cnt["p3sd0"])
            p.wait("gpsimd", "p3sd1", p.cnt["p3sd1"])
            GP(lambda E: E.collective_compute(
                "AllReduce", ALU.add,
                replica_groups=[[0, 1], [2, 3], [4, 5], [6, 7]],
                ins=[kvt_send[:, :].opt()], outs=[kvt_recv[:, :].opt()]),
               inc="p3cc")

        with contextlib.ExitStack() as _st:
            pb_t0b = _st.enter_context(nc.psum_tensor("pb_t0b", [128, 512], BF16))
            pb_t1b = _st.enter_context(nc.psum_tensor("pb_t1b", [128, 512], BF16))
            q_bf = _st.enter_context(nc.sbuf_tensor("q_bf", [128, NDK * NT], BF16))
            kvt_bf = _st.enter_context(nc.sbuf_tensor("kvt_bf", [128, 8 * 1024], BF16))
            wqc0 = _st.enter_context(nc.sbuf_tensor("wqc0", [128, 4096], BF16))
            wqc1 = _st.enter_context(nc.sbuf_tensor("wqc1", [128, 4096], BF16))
            zrow = _st.enter_context(nc.sbuf_tensor("zrow", [1, NT], F32))
            wsb = [wqc0, wqc1]
            nblk = 0
            nq = 0
            p.wait("gpsimd", "p3kv", 64)
            for cg in range(2):
                par = nblk % 2
                if nblk >= 2:
                    p.wait("gpsimd", "p3q", (nblk - 1) * 16)
                src = wq_in[:, cg * 512:(cg + 1) * 512]
                GPD(lambda E, src=src, buf=wsb[par]: E.dma_start(
                    out=buf[:, :].rearrange("p (k f) -> p k f", k=NDK),
                    in_=src.rearrange("(k p) f -> p k f", p=128)), inc=f"p3qw{par}")
                nblk += 1
                for cs in range(4):
                    for tg in range(4):
                        bank = ps_f[nq % 2]
                        p.wait("tensor", f"p3qw{par}", ((nblk - 1) // 2 + 1) * 16)
                        p.wait("tensor", "p3qe", nq - 1)
                        mm_acc(bank[:, :],
                               [(wsb[par][:, dk * 512 + cs * 128:dk * 512 + cs * 128 + 128],
                                 x_bf[:, dk * NT + tg * 512:dk * NT + tg * 512 + 512])
                                for dk in range(NDK)], inc="p3q")
                        qc = cg * 4 + cs
                        sb = nq % 2
                        p.wait("vector", "p3q", p.cnt["p3q"])
                        p.wait("vector", "p3qx", p.cnt.get("p3qx", 0))
                        VE(lambda E, b=bank, sb=sb: E.tensor_scalar_min(
                            sa_sb[sb][:, :], b[:, :], 0.0), inc="p3qmn")
                        p.wait("scalar", "p3qmn", p.cnt["p3qmn"])
                        SC(lambda E, sb=sb: E.activation(
                            tm_sb[sb][:, :], sa_sb[sb][:, :], AF.Exp), inc="p3qx")
                        p.wait("vector", "p3qx", p.cnt["p3qx"])
                        VE(lambda E, b=bank, sb=sb: E.tensor_scalar_max(
                            sa_sb[sb][:, :], b[:, :], 0.0), inc="p3qmx")
                        vsync()
                        qcol = qc * NT + tg * 512
                        VE(lambda E, sb=sb, qcol=qcol: E.scalar_tensor_tensor(
                            q_bf[:, qcol:qcol + 512], tm_sb[sb][:, :], 0.0,
                            sa_sb[sb][:, :], ALU.add, ALU.add), inc="p3qe")
                        nq += 1
            p.wait("gpsimd", "p3cc", 1)
            for dc in range(8):
                GPD(lambda E, dc=dc: E.dma_start(
                    out=kvt_bf[:, dc * 1024:(dc + 1) * 1024],
                    in_=kvt_recv[dc * 128:(dc + 1) * 128, 0:D]), inc="p3rb")
            GPD(lambda E: E.dma_start(
                out=ksb_col[:, :].rearrange("p (k c) -> p k c", c=1),
                in_=kvt_recv[:, D:D + 1].rearrange("(k p) c -> p k c", p=128),
                allow_slow_non_contiguous=True),
                inc="p3rb")
            for tg in range(4):
                bank = ps_f[4]
                p.wait("tensor", "p3rb", 9 * 16)
                p.wait("tensor", "p3qe", p.cnt["p3qe"])
                p.wait("tensor", "p3ze", p.cnt.get("p3ze", 0))
                mm_acc(bank[0:1, :],
                       [(ksb_col[:, dc:dc + 1],
                         q_bf[:, dc * NT + tg * 512:dc * NT + tg * 512 + 512])
                        for dc in range(8)], inc="p3z")
                p.wait("vector", "p3z", p.cnt["p3z"])
                c0 = tg * 512
                VE(lambda E, c0=c0: E.tensor_scalar_add(
                    zrow[:, c0:c0 + 512], ps_f[4][0:1, :], ATT_EPS), inc="p3ze")
            vsync()
            VE(lambda E: E.reciprocal(zrow[:, :], zrow[:, :]), inc="p3zr")
            nv = 0
            for tg in range(4):
                p.wait("tensor", "p3zr", 1)
                p.wait("tensor", "p3zc", tg)
                TE(lambda E, tg=tg: E.matmul(
                    ps_f[5][:, :], onesf_r[:, :], zrow[0:1, tg * 512:tg * 512 + 512],
                    start=True, stop=True), inc="p3zb")
                p.wait("vector", "p3zb", tg + 1)
                p.wait("vector", "p3ve", (tg - 1) * 8)
                VE(lambda E, zc=mbc_[tg % 2]: E.tensor_copy(
                    zc[:, :], ps_f[5][:, :]), inc="p3zc")
                for hc in range(8):
                    bank = ps_f[2 + (nv % 2)]
                    p.wait("tensor", "p3ve", nv - 1)
                    mm_acc(bank[:, :],
                           [(kvt_bf[:, dc * 1024 + hc * 128:dc * 1024 + hc * 128 + 128],
                             q_bf[:, dc * NT + tg * 512:dc * NT + tg * 512 + 512])
                            for dc in range(8)], inc="p3v")
                    sb = nv % 2
                    p.wait("vector", "p3v", p.cnt["p3v"])
                    p.wait("vector", "p3zc", tg + 1)
                    VE(lambda E, b=bank, sb=sb, zc=mbc_[tg % 2]: E.tensor_mul(
                        tm_sb[sb][:, :], b[:, :], zc[:, :]), inc="p3ve")
                    vsync()
                    xcol = hc * NT + tg * 512
                    VE(lambda E, sb=sb, xcol=xcol: E.tensor_add(
                        x_fm[:, xcol:xcol + 512], x_fm[:, xcol:xcol + 512],
                        tm_sb[sb][:, :]), inc="p3vr")
                    nv += 1
            p.wait("tensor", "p3vr", p.cnt["p3vr"])
        layer_norm("n3")
        p.barrier_release("p3", "n3")

    if PHASES >= 4:
        with nc.sbuf_tensor("phi_bf", [128, NDK * NP], BF16) as phi_bf:
            for dk in range(NDK):
                GPD(lambda E, dk=dk: E.dma_start(
                    out=phi_bf[:, dk * NP:(dk + 1) * NP],
                    in_=phi_in[dk * 128:(dk + 1) * 128, :]), inc="p4pw")
            for mt in range(16):
                for npg in range(4):
                    k = mt * 4 + npg
                    bank = ps_f[k % 2]
                    sb = k % 2
                    p.wait("tensor", "p4pw", 8 * 16)
                    p.wait("tensor", "p4ee", k - 1)
                    mm_acc(bank[:, :],
                           [(x_bf[:, dk * NT + mt * 128:dk * NT + mt * 128 + 128],
                             phi_bf[:, dk * NP + npg * 512:dk * NP + npg * 512 + 512])
                            for dk in range(NDK)], inc="p4lg")
                    p.wait("scalar", "p4lg", k + 1)
                    p.wait("scalar", "p4ca", k - 1)
                    if k >= 2:
                        p.wait("scalar", f"p4es{sb}", (k // 2) * 16)
                    SC(lambda E, b=bank, sb=sb: E.activation(
                        att_kv[sb][:, :], b[:, :], AF.Exp,
                        accum_out=kacc[:, 0:1]), inc="p4ee")
                    p.wait("vector", "p4ee", k + 1)
                    if npg == 0:
                        VE(lambda E, mt=mt: E.tensor_copy(
                            cols_a[:, mt:mt + 1], kacc[:, 0:1]), inc="p4ca")
                    else:
                        VE(lambda E, mt=mt: E.tensor_add(
                            cols_a[:, mt:mt + 1], cols_a[:, mt:mt + 1],
                            kacc[:, 0:1]), inc="p4ca")
                    p.wait("sync", "p4ee", k + 1)
                    SY(lambda E, mt=mt, npg=npg, sb=sb: E.dma_start(
                        out=e_send[mt * 128:(mt + 1) * 128, npg * 512:(npg + 1) * 512],
                        in_=att_kv[sb][:, :]), inc=f"p4es{sb}")
            p.wait("vector", "p4ca", 64)
            VE(lambda E: E.reciprocal(cols_a[:, :], cols_a[:, :]), inc="p4cr")
            p.wait("gpsimd", "p4es0", p.cnt["p4es0"])
            p.wait("gpsimd", "p4es1", p.cnt["p4es1"])
            GP(lambda E: E.collective_compute(
                "AllReduce", ALU.add,
                replica_groups=[[0, 2, 4, 6], [1, 3, 5, 7]],
                ins=[e_send[:, :].opt()], outs=[d_recv[:, :].opt()]),
               inc="p4cc1")

        with contextlib.ExitStack() as _st:
            pb_t0c = _st.enter_context(nc.psum_tensor("pb_t0c", [128, 512], BF16))
            pb_t1c = _st.enter_context(nc.psum_tensor("pb_t1c", [128, 512], BF16))
            x_tm = _st.enter_context(nc.sbuf_tensor("x_tm", [128, 16 * 1024], BF16))
            est_e = _st.enter_context(nc.sbuf_tensor("est_e", [128, 16 * 512], BF16))
            est_d = _st.enter_context(nc.sbuf_tensor("est_d", [128, 16 * 512], BF16))
            disp = _st.enter_context(nc.sbuf_tensor("disp", [128, 16 * 512], BF16))
            pb_tc = [pb_t0c, pb_t1c]
            nxt = 0
            for mt in range(16):
                for dc in range(NDK):
                    sb = nxt % 2
                    p.wait("tensor", "p4xe", nxt - 1)
                    TE(lambda E, mt=mt, dc=dc, b=pb_tc[sb]: E.transpose(
                        b[:, 0:128], x_bf[:, dc * NT + mt * 128:dc * NT + mt * 128 + 128],
                        ident_b[:, :]), inc="p4xt")
                    p.wait("vector", "p4xt", nxt + 1)
                    VE(lambda E, mt=mt, dc=dc, b=pb_tc[sb]: E.tensor_copy(
                        x_tm[:, mt * 1024 + dc * 128:mt * 1024 + dc * 128 + 128],
                        b[:, 0:128]), inc="p4xe")
                    nxt += 1
            for npg in range(4):
                p.wait("gpsimd", "p4cc1", 1)
                if npg:
                    p.wait("gpsimd", "p4dp", npg * 16)
                GPD(lambda E, npg=npg: E.dma_start(
                    out=est_d[:, :].rearrange("p (k c) -> p k c", k=16),
                    in_=d_recv[:, npg * 512:(npg + 1) * 512].rearrange(
                        "(k p) c -> p k c", p=128)), inc="p4dr")
                GPD(lambda E, npg=npg: E.dma_start(
                    out=est_e[:, :].rearrange("p (k c) -> p k c", k=16),
                    in_=e_send[:, npg * 512:(npg + 1) * 512].rearrange(
                        "(k p) c -> p k c", p=128)), inc="p4er")
                for lm in range(16):
                    sb = lm % 2
                    if lm == 0 and npg > 0:
                        p.wait("vector", "p4sl", npg * 8)
                    p.wait("scalar", "p4dr", (npg + 1) * 16)
                    p.wait("scalar", "p4dp", npg * 16 + lm - 1)
                    SC(lambda E, lm=lm, sb=sb: E.activation(
                        tm_sb[sb][:, :], est_d[:, lm * 512:(lm + 1) * 512],
                        AF.Ln), inc="p4ln")
                    p.wait("scalar", "p4ln", p.cnt["p4ln"])
                    SC(lambda E, sb=sb: E.activation(
                        sa_sb[sb][:, :], tm_sb[sb][:, :], AF.Exp, scale=-1.0),
                       inc="p4ex")
                    p.wait("vector", "p4ex", p.cnt["p4ex"])
                    p.wait("vector", "p4er", (npg + 1) * 16)
                    VE(lambda E, lm=lm, sb=sb: E.tensor_mul(
                        disp[:, lm * 512:(lm + 1) * 512],
                        est_e[:, lm * 512:(lm + 1) * 512], sa_sb[sb][:, :]),
                       inc="p4dp")
                for dc in range(NDK):
                    k = npg * 8 + dc
                    bank = ps_f[2 + (k % 2)]
                    sb = k % 2
                    p.wait("tensor", "p4dp", (npg + 1) * 16)
                    p.wait("tensor", "p4xe", 128)
                    p.wait("tensor", "p4se", k - 1)
                    mm_acc(bank[:, :],
                           [(x_tm[:, mt * 1024 + dc * 128:mt * 1024 + dc * 128 + 128],
                             disp[:, mt * 512:(mt + 1) * 512]) for mt in range(16)],
                           inc="p4sl")
                    p.wait("vector", "p4sl", k + 1)
                    if k >= 2:
                        p.wait("vector", f"p4ss{sb}", (k // 2) * 16)
                    VE(lambda E, b=bank, sb=sb: E.tensor_copy(
                        att_kv[sb][:, :], b[:, :]), inc="p4se")
                    p.wait("sync", "p4se", k + 1)
                    SY(lambda E, dc=dc, npg=npg, sb=sb: E.dma_start(
                        out=s_send[dc * 128:(dc + 1) * 128, npg * 512:(npg + 1) * 512],
                        in_=att_kv[sb][:, :]), inc=f"p4ss{sb}")
            p.wait("gpsimd", "p4ss0", p.cnt["p4ss0"])
            p.wait("gpsimd", "p4ss1", p.cnt["p4ss1"])
            GP(lambda E: E.collective_compute(
                "AllReduce", ALU.add,
                replica_groups=[[0, 1], [2, 3], [4, 5], [6, 7]],
                ins=[s_send[:, :].opt()], outs=[s_recv[:, :].opt()]),
               inc="p4cc2")

        slt_bf = x_bf
        with contextlib.ExitStack() as _sty:
          y_bf = _sty.enter_context(nc.sbuf_tensor("y_bf", [128, 16 * 1024], BF16))
          with contextlib.ExitStack() as _st:
            wfb0 = _st.enter_context(nc.sbuf_tensor("wfb0", [128, 8192], BF16))
            wfb1 = _st.enter_context(nc.sbuf_tensor("wfb1", [128, 8192], BF16))
            wsb = [wfb0, wfb1]
            p.wait("gpsimd", "p4cc2", 1)
            for dc in range(NDK):
                GPD(lambda E, dc=dc: E.dma_start(
                    out=slt_bf[:, dc * NP:(dc + 1) * NP],
                    in_=s_recv[dc * 128:(dc + 1) * 128, :]), inc="p4sr")
            for n in range(64):
                par = n % 2
                if n >= 2:
                    p.wait("gpsimd", "p4yg", (n - 1) * 2)
                p.wait("gpsimd", "wfc", 64 * 16)
                GPD(lambda E, n=n, buf=wsb[par]: E.dma_start(
                    out=buf[:, :].rearrange("p (k e) -> p k e", k=NDK),
                    in_=wf_bf[n, :, :].rearrange("(k p) e -> p k e", p=128)),
                    inc=f"p4wf{par}")
            for n in range(64):
                par = n % 2
                j = n % 4
                q = n // 4
                for eg in range(2):
                    bank = ps_f[2 + 2 * (q % 2) + eg]
                    if n == 0 and eg == 0:
                        p.wait("tensor", "p4sr", 8 * 16)
                    p.wait("tensor", f"p4wf{par}", (n // 2 + 1) * 16)
                    if q >= 2 and eg == 0 and j == 0:
                        p.wait("tensor", "p4ye", (q - 1) * 8)
                    mm_acc(bank[32 * j:32 * (j + 1), :],
                           [(slt_bf[:, dk * NP + n * 32:dk * NP + n * 32 + 32],
                             wsb[par][:, dk * 1024 + eg * 512:dk * 1024 + eg * 512 + 512])
                            for dk in range(NDK)], inc="p4yg",
                           tile_position=(0, 32 * j))
                if j == 3:
                    p.wait("vector", "p4yg", (n + 1) * 2)
                    for jj in range(4):
                        for eg in range(2):
                            bank = ps_f[2 + 2 * (q % 2) + eg]
                            nn = q * 4 + jj
                            VE(lambda E, b=bank, jj=jj, q=q, eg=eg: E.tensor_copy(
                                y_bf[32 * jj:32 * (jj + 1),
                                     q * 1024 + eg * 512:q * 1024 + eg * 512 + 512],
                                b[32 * jj:32 * (jj + 1), :]), inc="p4ye")

          with contextlib.ExitStack() as _st:
                pb_t0d = _st.enter_context(nc.psum_tensor("pb_t0d", [128, 512], BF16))
                pb_t1d = _st.enter_context(nc.psum_tensor("pb_t1d", [128, 512], BF16))
                est2 = _st.enter_context(nc.sbuf_tensor("est2", [128, 4 * NP], BF16))
                comb_t = _st.enter_context(nc.sbuf_tensor("comb_t", [128, 16 * 512], BF16))
                pb_td = [pb_t0d, pb_t1d]
                for eng in ("gpsimd", "vector", "tensor", "scalar", "sync"):
                    for nm in ("p4wf0", "p4wf1", "p4yg", "p4ye", "p4sr", "p4xe"):
                        p.wait(eng, nm, p.cnt.get(nm, 0))
                for mg in range(4):
                    if mg:
                        p.wait("gpsimd", "p4ob", mg * 8)
                    GPD(lambda E, mg=mg: E.dma_start(
                        out=est2[:, :].rearrange("p (k c) -> p k c", k=4),
                        in_=e_send[mg * 512:(mg + 1) * 512, :].rearrange(
                            "(k p) c -> p k c", p=128)), inc="p4e2")
                    nct = 0
                    for lm in range(4):
                        mt = mg * 4 + lm
                        for npg in range(4):
                            sb = nct % 2
                            p.wait("vector", "p4e2", (mg + 1) * 16)
                            p.wait("vector", "p4cr", 1)
                            p.wait("vector", "p4tt", p.cnt.get("p4tt", 0))
                            VE(lambda E, lm=lm, npg=npg, sb=sb, mt=mt:
                               E.tensor_scalar(
                                att_kv[sb][:, :],
                                est2[:, lm * NP + npg * 512:lm * NP + npg * 512 + 512],
                                cols_a[:, mt:mt + 1], None, ALU.mult), inc="p4cm")
                            p.wait("tensor", "p4cm", p.cnt["p4cm"])
                            p.wait("tensor", "p4ct", p.cnt.get("p4ct", 0) - 4)
                            for jj in range(4):
                                TE(lambda E, jj=jj, sb=sb, b=pb_td[sb]: E.transpose(
                                    b[:, jj * 128:(jj + 1) * 128],
                                    att_kv[sb][:, jj * 128:(jj + 1) * 128],
                                    ident_b[:, :]), inc="p4tt" if jj == 3 else None)
                            p.wait("vector", "p4tt", p.cnt["p4tt"])
                            for jj in range(4):
                                ccol = (npg * 4 + jj) * 512 + lm * 128
                                VE(lambda E, jj=jj, sb=sb, b=pb_td[sb], ccol=ccol:
                                   E.tensor_copy(comb_t[:, ccol:ccol + 128],
                                                 b[:, jj * 128:(jj + 1) * 128]),
                                   inc="p4ct")
                            nct += 1
                    p.wait("tensor", "p4ct", p.cnt["p4ct"])
                    p.wait("tensor", "bo2", 16)
                    p.wait("tensor", "p4ye", 128)
                    p.wait("tensor", "p4cne", mg)
                    mm_acc(ps_f[4][0:64, :],
                           [(bo_bf[:, npc * 64:(npc + 1) * 64],
                             comb_t[:, npc * 512:(npc + 1) * 512])
                            for npc in range(16)], inc="p4cn")
                    p.wait("vector", "p4cn", p.cnt["p4cn"])
                    VE(lambda E: E.tensor_copy(cn_sb[:, :], ps_f[4][0:64, :]),
                       inc="p4cne")
                    for dc in range(NDK):
                        k = mg * 8 + dc
                        bank = ps_f[k % 2]
                        p.wait("tensor", "p4cne", mg + 1)
                        p.wait("tensor", "bfll", 16)
                        p.wait("tensor", "p4ob", k - 1)
                        ops = [(bfl_bf[:, dc * 128:(dc + 1) * 128], cn_sb[:, :])]
                        ops += [(y_bf[:, npc * 1024 + dc * 128:npc * 1024 + dc * 128 + 128],
                                 comb_t[:, npc * 512:(npc + 1) * 512])
                                for npc in range(16)]
                        mm_acc(bank[:, :], ops, inc="p4ou")
                        p.wait("vector", "p4ou", k + 1)
                        xcol = dc * NT + mg * 512
                        VE(lambda E, b=bank, xcol=xcol: E.tensor_add(
                            x_fm[:, xcol:xcol + 512], x_fm[:, xcol:xcol + 512],
                            b[:, :]), inc="p4ob")
                p.wait("tensor", "p4ob", p.cnt["p4ob"])
        layer_norm("n4")
        p.barrier_release("p4", "n4")

    not_ = 0
    with contextlib.ExitStack() as _st:
      stgo = [_st.enter_context(nc.sbuf_tensor(f"stgo{i}", [128, 1024], F32))
              for i in range(2)]
      for mt in range(16):
        b = mt % 2
        for dk in range(NDK):
            pb = not_ % 2
            p.wait("tensor", "oevac", not_ - 1)
            col = dk * NT + mt * 128
            TE(lambda E, pb=pb, col=col: E.transpose(
                ps_f[pb][:, 0:128], x_fm[:, col:col + 128], ident_f[:, :]),
               inc="otr")
            p.wait("vector", "otr", not_ + 1)
            if mt >= 2 and dk == 0:
                p.wait("vector", f"ost{b}", (mt // 2) * 16)
            VE(lambda E, pb=pb, b=b, dk=dk, t=stgo[b]: E.tensor_copy(
                t[:, dk * 128:(dk + 1) * 128], ps_f[pb][:, 0:128]),
               inc="oevac")
            not_ += 1
        p.wait("sync", "oevac", not_)
        SY(lambda E, b=b, mt=mt, t=stgo[b]: E.dma_start(
            out=y_out[mt * 128:(mt + 1) * 128, :], in_=t[:, :]),
           inc=f"ost{b}")
      p.wait("sync", "ost0", 8 * 16)
      p.wait("sync", "ost1", 8 * 16)

    p.finalize()
    return nc


def emit_phase4(nc, p, L):
    raise NotImplementedError


_NC_CACHE = {}


def kernel(**inputs):
    x = np.ascontiguousarray(np.asarray(inputs["L0_attn_x"], dtype=np.float32))
    rw = np.ascontiguousarray(np.asarray(inputs["L0_ffn_router_w"], dtype=np.float32))
    rb = np.ascontiguousarray(np.asarray(inputs["L0_ffn_router_b"], dtype=np.float32)).reshape(1, NE)
    w1 = np.ascontiguousarray(np.asarray(inputs["L0_ffn_w1"], dtype=np.float32))
    w2 = np.ascontiguousarray(np.asarray(inputs["L0_ffn_w2"], dtype=np.float32))
    wq = np.ascontiguousarray(np.asarray(inputs["L1_attn_W"], dtype=np.float32))
    phi = np.ascontiguousarray(np.asarray(inputs["L1_ffn_phi"], dtype=np.float32)).reshape(D, NP)
    wf = np.ascontiguousarray(np.asarray(inputs["L1_ffn_W"], dtype=np.float32))
    bfb = np.ascontiguousarray(np.asarray(inputs["L1_ffn_b"], dtype=np.float32))

    if "nc" not in _NC_CACHE:
        _NC_CACHE["nc"] = build(0)
    ncp = _NC_CACHE["nc"]

    in_maps = []
    for c in range(8):
        b, h = c // 2, c % 2
        in_maps.append({
            "x": np.ascontiguousarray(x[b, h * NT:(h + 1) * NT, :]),
            "rw": rw, "rb": rb, "w1": w1, "w2": w2,
            "wq": wq, "phi": phi, "wf": wf, "bfb": bfb,
        })
    res = run_bass_kernel_spmd(ncp, in_maps, core_ids=list(range(8)))
    out = np.empty((4, 4096, D), dtype=np.float32)
    for c in range(8):
        b, h = c // 2, c % 2
        out[b, h * NT:(h + 1) * NT, :] = res.results[c]["y"]
    return out
